# version 13
# baseline (speedup 1.0000x reference)
import contextlib
import numpy as np

import concourse.bass as bass
import concourse.mybir as mybir
from concourse.bass_utils import run_bass_kernel_spmd

F32 = mybir.dt.float32
BF16 = mybir.dt.float16
AF = mybir.ActivationFunctionType
OP = mybir.AluOpType
AXX = mybir.AxisListType.X

ENGINES = ("sync", "tensor", "vector", "scalar", "gpsimd")
SAFE_EVERY = 0
_LAST_PLAN = []
T, V, A, S, G = "tensor", "vector", "scalar", "sync", "gpsimd"


class Cfg:
    def __init__(self, nseq=4, s=2048, d=1024, dff=2816, ncores=8):
        self.NSEQ = nseq
        self.S = s
        self.D = d
        self.DFF = dff
        self.NC = ncores
        self.FC = d // 128
        self.IC = dff // 128
        self.TT = s // 128
        self.KC = s // 128
        self.NH = d // 64
        self.TOK = nseq * s
        self.NT16 = self.TOK // 128
        self.FBLK = min(512, self.TOK)
        self.NBLK = self.TOK // self.FBLK
        self.FSP = 256 if self.TOK >= 512 else self.TOK
        self.NSP = self.TOK // self.FSP


class Plan:

    def __init__(self):
        self.ops = []
        self.cnt = {}
        self.sem_names = []
        self.eng_sem = {}
        self.pending = {}

    def sem(self, name):
        self.sem_names.append(name)
        self.cnt[name] = 0
        return name

    def op(self, eng, fn, waits=(), inc=None, amt=1):
        waits = tuple(w for w in waits if w is not None)
        pend = self.pending.pop(eng, None)
        if pend is not None:
            waits = waits + (pend,)
        tok = None
        if inc is not None:
            self.cnt[inc] += amt
            tok = (inc, self.cnt[inc])
        self.ops.append((eng, fn, waits, (inc, amt) if inc else None))
        return tok

    def dma(self, eng, ch, out, in_, waits=()):
        return self.op(eng, lambda e, o=out, i=in_: e.dma_start(out=o, in_=i),
                       waits=waits, inc=ch, amt=16)

    def drain(self, eng):
        sem = self.eng_sem.get(eng)
        if sem is None or self.cnt[sem] == 0:
            self.op(eng, lambda e: e.drain())
            return
        self.pending[eng] = (sem, self.cnt[sem])

    def barrier(self, bar):
        vals = {}
        for e in ENGINES:
            self.cnt[bar[e]] += 1
            vals[e] = self.cnt[bar[e]]
            self.ops.append((e, lambda eng: eng.nop(), (), (bar[e], 1)))
        for e in ENGINES:
            waits = tuple((bar[o], vals[o]) for o in ENGINES if o != e)
            self.ops.append((e, lambda eng: eng.nop(), waits, None))

    def compress(self, sems_to_strip):
        waited = {}
        for _, _, waits, _ in self.ops:
            for s_, v_ in waits:
                waited.setdefault(s_, set()).add(v_)
        cnt = {s_: 0 for s_ in self.cnt}
        kept = {s_: 0 for s_ in self.cnt}
        remap = {}
        newops = []
        for eng, fn, waits, inc in self.ops:
            w2 = tuple((s_, remap.get((s_, v_), v_)) for s_, v_ in waits)
            inc2 = inc
            if inc is not None:
                s_, amt = inc
                cnt[s_] += amt
                if s_ in sems_to_strip:
                    if cnt[s_] in waited.get(s_, ()):
                        kept[s_] += amt
                        inc2 = (s_, amt)
                    else:
                        inc2 = None
                    remap[(s_, cnt[s_])] = kept[s_]
                else:
                    kept[s_] += amt
            newops.append((eng, fn, w2, inc2))
        self.ops = newops

    def check(self):
        q = {e: [i for i, rec in enumerate(self.ops) if rec[0] == e] for e in ENGINES}
        pos = {e: 0 for e in ENGINES}
        sem = {s_: 0 for s_ in self.cnt}
        progress = True
        while progress:
            progress = False
            for e in ENGINES:
                while pos[e] < len(q[e]):
                    rec = self.ops[q[e][pos[e]]]
                    if any(sem[s_] < v for s_, v in rec[2]):
                        break
                    if rec[3]:
                        sem[rec[3][0]] += rec[3][1]
                    pos[e] += 1
                    progress = True
        stuck = {e: len(q[e]) - pos[e] for e in ENGINES if pos[e] < len(q[e])}
        if stuck:
            details = []
            for e in stuck:
                rec = self.ops[q[e][pos[e]]]
                missing = [(s_, v, sem[s_]) for s_, v in rec[2] if sem[s_] < v]
                details.append(f"{e}@{pos[e]}/{len(q[e])} waiting {missing}")
            raise RuntimeError("plan deadlock: " + "; ".join(details))


def build_nc(cfg: Cfg):
    nc = bass.Bass("TRN2", target_bir_lowering=False)
    NSEQ, SL, D, DFF = cfg.NSEQ, cfg.S, cfg.D, cfg.DFF
    FC, IC, TT, KC, NH = cfg.FC, cfg.IC, cfg.TT, cfg.KC, cfg.NH
    TOK, NT16 = cfg.TOK, cfg.NT16
    FBLK, NBLK, NSP, FSP = cfg.FBLK, cfg.NBLK, cfg.NSP, cfg.FSP
    NHP = NH // 2
    OSW = min(512, D)

    x_d = nc.dram_tensor("x", [NSEQ, SL, D], F32, kind="ExternalInput").ap()
    w_attn_d = [nc.dram_tensor(n, [D, D], F32, kind="ExternalInput").ap()
                for n in ("wq", "wk", "wv", "wo")]
    wg_d = nc.dram_tensor("wg", [DFF, D], F32, kind="ExternalInput").ap()
    wu_d = nc.dram_tensor("wu", [DFF, D], F32, kind="ExternalInput").ap()
    wd_d = nc.dram_tensor("wd", [D, DFF], F32, kind="ExternalInput").ap()
    clipx_d = nc.dram_tensor("clipx", [1, 1], F32, kind="ExternalInput").ap()
    clipd_d = nc.dram_tensor("clipd", [1, 1], F32, kind="ExternalInput").ap()
    ident_d = nc.dram_tensor("ident", [128, 128], F32, kind="ExternalInput").ap()
    mask_d = nc.dram_tensor("mask", [128, 128], F32, kind="ExternalInput").ap()
    consts_d = nc.dram_tensor("consts", [128, 8], F32, kind="ExternalInput").ap()
    ones_d = nc.dram_tensor("ones_row", [1, 128], F32, kind="ExternalInput").ap()
    y_d = nc.dram_tensor("y", [NSEQ, SL, D], F32, kind="ExternalOutput").ap()

    wt_scr = [nc.dram_tensor(f"wt{i}", [FC, FC, 128, 128], F32).ap()
              for i in range(4)]
    wgq_scr = nc.dram_tensor("wgq", [FC, IC, 128, 128], BF16).ap()
    wuq_scr = nc.dram_tensor("wuq", [FC, IC, 128, 128], BF16).ap()
    wdq_scr = nc.dram_tensor("wdq", [IC, FC, 128, 128], BF16).ap()
    attn_scr = nc.dram_tensor("attns", [NH, TT, 128, 64], F32).ap()
    r1_scr = nc.dram_tensor("r1s", [NT16, 128, D], F32).ap()
    h_scr = nc.dram_tensor("hs", [IC, 128, TOK], F32).ap()
    cc_io = [nc.dram_tensor(f"cc{i}", [1, 1], F32).ap() for i in range(4)]

    CB_EXP, CB_EPS, CB_HALF, CB_ONE, CB_ZERO = 0, 1, 2, 3, 5

    p = Plan()
    ld_c = p.sem("ld_c")
    ld_p = [p.sem(f"ld_p{i}") for i in range(2)]
    st_p = [p.sem(f"st_p{i}") for i in range(2)]
    ld_w = [p.sem(f"ld_w{i}") for i in range(2)]
    ld_x = [p.sem(f"ld_x{i}") for i in range(2)]
    ld_r1 = [p.sem(f"ld_r1{i}") for i in range(2)]
    ld_h = [p.sem(f"ld_h{i}") for i in range(2)]
    ld_at = [p.sem(f"ld_at{i}") for i in range(2)]
    st_at = [p.sem(f"st_at{i}") for i in range(2)]
    st_r1 = p.sem("st_r1")
    st_h = [p.sem(f"st_h{i}") for i in range(2)]
    st_y = [p.sem(f"st_y{i}") for i in range(2)]
    ccch = p.sem("ccch"); ccd = p.sem("ccd")
    ts = p.sem("ts"); vs = p.sem("vs"); as_ = p.sem("as_")
    p.eng_sem = {T: ts, V: vs, A: as_}
    bar = {e: p.sem("bar_" + e) for e in ENGINES}

    with contextlib.ExitStack() as stk:
        ent = stk.enter_context

        def sbt(name, shape, dt=F32):
            return ent(nc.sbuf_tensor(name, shape, dt))

        AW = max(FC * SL, 2 * DFF, 4 * D)
        arena = sbt("arena", [128, AW])
        xT = arena[:, 0:FC * SL].rearrange("p (a b) -> p a b", a=FC)
        qt = sbt("qt", [128, 2 * SL])
        kt = sbt("kt", [128, 2 * SL])
        v65 = sbt("v65", [128, KC, 130])
        e_sb = [sbt(f"e{i}", [128, SL]) for i in range(2)]
        ws = sbt("ws", [128, 2, FC, 128])
        use_gaps = (FC >= 6 and SL >= 2 * D and AW >= 11 * D)
        n1k = 6 if use_gaps else 12
        s1k = [sbt(f"s1k{i}", [128, max(1024, D)]) for i in range(n1k)]
        if use_gaps:
            a2gap = [arena[:, i * SL + D:(i + 1) * SL] for i in range(6)]
            f2v = [arena[:, (4 + i) * D:(5 + i) * D] for i in range(7)]
        else:
            a2gap = [s1k[6], s1k[7], s1k[8], s1k[9], s1k[0], s1k[1]]
            f2v = [s1k[5], s1k[6], s1k[7], s1k[8], s1k[9], s1k[10], s1k[11]]
        s512 = [sbt(f"s512{i}", [128, 512]) for i in range(6)]
        ssm = [sbt(f"ssm{i}", [128, 1]) for i in range(8)]
        rmax_all = sbt("rmax_all", [128, max(NT16, 2)])
        nhc = max(IC * (FBLK // 512) * NBLK, 2)
        hmax_all = sbt("hmax_all", [128, nhc])
        sg_sb = sbt("sg", [128, IC]); su_sb = sbt("su", [128, IC])
        sd_sb = sbt("sd", [128, FC])
        gsc_sb = sbt("gsc", [128, IC]); usc_sb = sbt("usc", [128, IC])
        dsc_sb = sbt("dsc", [128, FC])
        consts = sbt("consts_sb", [128, 8])
        ident = sbt("ident_sb", [128, 128])
        ident_bf = sbt("identbf_sb", [128, 128], BF16)
        mask = sbt("mask_sb", [128, 128])
        ones_row = sbt("ones_sb", [1, 128])
        scx = sbt("scx", [1, 1]); scd_ = sbt("scd", [1, 1])
        sig_cx = sbt("sigcx", [1, 1]); sig_cd = sbt("sigcd", [1, 1])
        lmax = sbt("lmax", [1, 1])
        gmax = [sbt(f"gmax{i}", [1, 1]) for i in range(2)]
        tmp11 = [sbt(f"t11_{i}", [1, 1]) for i in range(4)]
        sx_bc = sbt("sx_bc", [128, 1]); rsx_bc = sbt("rsx_bc", [128, 1])
        s2_bc = sbt("s2_bc", [128, 1]); rq2_bc = sbt("rq2_bc", [128, 1])
        bfw = max(2 * FC * FBLK + 4 * FC * 128,
                  2 * IC * FSP + 2 * IC * 128, 9216 + 32)
        bfa = sbt("bfa", [128, bfw], BF16)
        fpw = sbt("fpw", [128, D])
        i32t = sbt("i32t", [128, 512], mybir.dt.int32)

        NXQ = 3 if 2 * FC * FBLK + 4 * FC * 128 + FC * FBLK <= bfw else 2
        xqT = [bfa[:, i * FC * FBLK:(i + 1) * FC * FBLK].rearrange(
            "p (a b) -> p a b", a=FC) for i in range(NXQ)]
        wslg = bfa[:, NXQ * FC * FBLK:NXQ * FC * FBLK + 4 * FC * 128].rearrange(
            "p (a b c) -> p a b c", a=2, c=128)
        NHQ = 3 if 3 * IC * FSP + 2 * IC * 128 <= bfw else 2
        hqT = [bfa[:, i * IC * FSP:(i + 1) * IC * FSP].rearrange(
            "p (a b) -> p a b", a=IC) for i in range(NHQ)]
        wsld = bfa[:, NHQ * IC * FSP:NHQ * IC * FSP + 2 * IC * 128].rearrange(
            "p (a b c) -> p a b c", a=2, c=128)
        wqstg = bfa[:, 0:1024]
        qstg = bfa[:, 8192:9216] if bfw >= 9216 else bfa[:, 1024:2048]

        astg = [s512[4][:, 0:64], s512[5][:, 0:64]]
        pw = [arena[:, i * DFF:(i + 1) * DFF] for i in range(2)]
        wo_view = xT
        outpre = arena[:, 0:4 * D].rearrange("p (a b) -> p a b", a=4)

        p_big = [ent(nc.psum_tensor(f"pb{i}", [128, 512], F32)) for i in range(2)]
        p_pv = ent(nc.psum_tensor("ppv", [128, KC * 128], F32))
        p_t = ent(nc.psum_tensor("pt", [128, 512], F32))
        p_misc = ent(nc.psum_tensor("pm", [128, 512], F32))

        sems = {nm: ent(nc.semaphore(nm)) for nm in p.sem_names}

        pb_last = {0: None, 1: None}
        pt_last = {k: None for k in range(4)}
        pm_last = [None]

        def _ap(x):
            return x if isinstance(x, bass.AP) else x[:]

        def mm(out, lhsT, rhs, start, stop, waits=(), skip=False):
            out, lhsT, rhs = _ap(out), _ap(lhsT), _ap(rhs)
            return p.op(T, lambda e, o=out, l=lhsT, r=rhs, s1=start, s2=stop,
                        sk=skip: e.matmul(o, l, r, start=s1, stop=s2,
                                          skip_group_check=sk),
                        waits=waits, inc=ts)

        def tr_(out, in_, waits=()):
            out, in_ = _ap(out), _ap(in_)
            idn = ident_bf[:] if in_.dtype == BF16 else ident[:]
            return p.op(T, lambda e, o=out, i=in_, n=idn: e.transpose(o, i, n),
                        waits=waits, inc=ts)

        def vcopy(out, in_, waits=()):
            out, in_ = _ap(out), _ap(in_)
            return p.op(V, lambda e, o=out, i=in_: e.tensor_copy(out=o, in_=i),
                        waits=waits, inc=vs)

        def vts(out, in0, s1, s2=None, op0=OP.mult, op1=None, waits=()):
            out, in0 = _ap(out), _ap(in0)
            s1 = s1 if isinstance(s1, (int, float)) else _ap(s1)
            if op1 is None:
                return p.op(V, lambda e, o=out, i=in0, a=s1, c0=op0:
                            e.tensor_scalar(out=o, in0=i, scalar1=a, scalar2=None,
                                            op0=c0), waits=waits, inc=vs)
            return p.op(V, lambda e, o=out, i=in0, a=s1, b=s2, c0=op0, c1=op1:
                        e.tensor_scalar(out=o, in0=i, scalar1=a, scalar2=b,
                                        op0=c0, op1=c1), waits=waits, inc=vs)

        def vtt(out, in0, in1, op, waits=()):
            out, in0, in1 = _ap(out), _ap(in0), _ap(in1)
            return p.op(V, lambda e, o=out, a=in0, b=in1, c=op:
                        e.tensor_tensor(out=o, in0=a, in1=b, op=c),
                        waits=waits, inc=vs)

        def vred(out, in_, op=OP.max, absv=False, waits=()):
            out, in_ = _ap(out), _ap(in_)
            return p.op(V, lambda e, o=out, i=in_, c=op, av=absv:
                        e.tensor_reduce(out=o, in_=i, axis=AXX, op=c,
                                        apply_absolute_value=av),
                        waits=waits, inc=vs)

        def vrecip(out, in_, waits=()):
            out, in_ = _ap(out), _ap(in_)
            return p.op(V, lambda e, o=out, i=in_: e.reciprocal(out=o, in_=i),
                        waits=waits, inc=vs)

        def act(out, in_, fn, bias=None, scale=1.0, waits=()):
            out, in_ = _ap(out), _ap(in_)
            if bias is None:
                return p.op(A, lambda e, o=out, i=in_, f=fn, sc=scale:
                            e.activation(out=o, in_=i, func=f, scale=sc),
                            waits=waits, inc=as_)
            b_ap = _ap(bias)
            return p.op(A, lambda e, o=out, i=in_, f=fn, b=b_ap, sc=scale:
                        e.activation(out=o, in_=i, func=f, bias=b, scale=sc),
                        waits=waits, inc=as_)

        def vstt(out, in0, scalar, in1, op0, op1, waits=()):
            out, in0, in1 = _ap(out), _ap(in0), _ap(in1)
            scalar = scalar if isinstance(scalar, (int, float)) else _ap(scalar)
            return p.op(V, lambda e, o=out, a=in0, s=scalar, b=in1, c0=op0,
                        c1=op1: e.scalar_tensor_tensor(
                            out=o, in0=a, scalar=s, in1=b, op0=c0, op1=c1),
                        waits=waits, inc=vs)

        def acopy(out, in_, waits=()):
            out, in_ = _ap(out), _ap(in_)
            return p.op(A, lambda e, o=out, i=in_: e.copy(out=o, in_=i),
                        waits=waits, inc=as_)

        def vmemset(ap_, val, waits=()):
            ap_ = _ap(ap_)
            return p.op(V, lambda e, a=ap_, v=val: e.memset(a, v),
                        waits=waits, inc=vs)

        def tslot(slot):
            t_ = p_t if slot % 2 == 0 else p_misc
            off = (slot // 2) * 128
            return t_[:, off:off + 128]

        def transpose_to(slot, dst, src, waits=()):
            sl = tslot(slot)
            other = (slot + 2) % 4
            tk = tr_(sl, src, waits=tuple(waits) +
                     (pt_last[slot], pt_last[other], tk_i, tk_ibf))
            tv = vcopy(dst, sl, waits=(tk,))
            pt_last[slot] = tv
            return tv

        def quant_chain(dst_bf, src, rq_ap, tmp_a, tmp_b, tmp_c, i32_ap,
                        waits=()):
            p.drain(V)
            tu = vts(tmp_a, src, rq_ap, 0.5, OP.mult, OP.add, waits=waits)
            t1 = acopy(i32_ap, tmp_a, waits=(tu,))
            p.drain(A)
            t2 = acopy(tmp_b, i32_ap)
            td = vtt(tmp_c, tmp_a, tmp_b, OP.subtract, waits=(t2,))
            p.drain(V)
            vstt(tmp_b, tmp_c, -1.0, tmp_b, OP.is_ge, OP.add)
            p.drain(V)
            vstt(tmp_b, tmp_c, 0.0, tmp_b, OP.is_ge, OP.add)
            p.drain(V)
            vstt(tmp_b, tmp_c, 1.0, tmp_b, OP.is_ge, OP.add)
            p.drain(V)
            vts(tmp_a, tmp_b, -2.0, 127.0, OP.add, OP.min)
            p.drain(V)
            return vts(dst_bf, tmp_a, -128.0, None, OP.max)

        def ln_chain(src, y_out, tc_b, sq_b, waits=(), prev=None):
            tc_b = _ap(tc_b)[:, 0:D]
            sq_b = _ap(sq_b)[:, 0:D]
            w0 = tuple(waits) + ((prev,) if prev else ())
            p.drain(V)
            t0 = vred(ssm[0], src, OP.add, waits=w0)
            p.drain(V)
            t1 = vts(ssm[1], ssm[0], -1.0 / D, None, OP.mult)
            p.drain(V)
            tA = vts(tc_b, src, ssm[1][:, 0:1], None, OP.add, waits=w0)
            tB = act(sq_b, tc_b, AF.Square, waits=(tA,))
            t2 = vred(ssm[2], sq_b, OP.add, waits=(tB,))
            tC = act(ssm[3], ssm[2], AF.Sqrt, bias=consts[:, CB_EPS:CB_EPS + 1],
                     scale=1.0 / D, waits=(t2,) + ((prev,) if prev else ()))
            t3 = vrecip(ssm[4], ssm[3], waits=(tC,))
            p.drain(V)
            t4 = vts(y_out, tc_b, ssm[4][:, 0:1], None, OP.mult, waits=(tA,))
            return t4, tA

        p.dma(S, ld_c, consts[:], consts_d)
        p.dma(S, ld_c, ident[:], ident_d)
        p.dma(S, ld_c, mask[:], mask_d)
        p.dma(S, ld_c, ones_row[:], ones_d)
        p.dma(S, ld_c, scx[:], clipx_d)
        tk_in = p.dma(S, ld_c, scd_[:], clipd_d)
        tk_i = tk_m = tk_o = tk_in

        vmemset(qt[64:128, 0:SL], 0.0)
        vmemset(qt[0:64, SL:2 * SL], 0.0)
        vmemset(kt[64:128, 0:SL], 0.0)
        vmemset(kt[0:64, SL:2 * SL], 0.0)
        for kc in range(KC):
            vmemset(v65[:, kc, 64:65], 1.0)
            vmemset(v65[:, kc, 129:130], 1.0)
        tk_ibf = vcopy(ident_bf, ident, waits=(tk_i,))
        tk_sig1 = act(sig_cx[:], scx[:], AF.Sigmoid, waits=(tk_in,))
        p.drain(A)
        tk_sig2 = act(sig_cd[:], scd_[:], AF.Sigmoid, waits=(tk_in,))

        stg_tok = {0: None, 1: None}
        stg_i = [0]

        def stage_store(src_slot_tv_fn, dst, bf):
            b = stg_i[0] % 2
            stg_i[0] += 1
            stg = (wqstg[:, b * 512:b * 512 + 128] if bf
                   else s512[4 + b][:, 0:128])
            tv = src_slot_tv_fn(stg, stg_tok[b])
            td = p.dma(G, st_p[b], dst, stg, waits=(tv,))
            stg_tok[b] = td
            return td

        pw_tok = [None, None]
        for wi in range(4):
            for oc in range(FC):
                b = oc % 2
                ld = p.dma(S, ld_p[b], pw[b][:, 0:D],
                           w_attn_d[wi][oc * 128:oc * 128 + 128, :],
                           waits=(pw_tok[b],))
                for fc in range(FC):
                    slot = fc % 4
                    sl = tslot(slot)
                    tk = tr_(sl, pw[b][:, fc * 128:fc * 128 + 128],
                             waits=(ld, pt_last[slot], pt_last[(slot + 2) % 4],
                                    tk_i))

                    def mk(sl_=sl, tk_=tk):
                        def f(stg, st_tok):
                            return vcopy(stg, sl_, waits=(tk_, st_tok))
                        return f
                    tv_holder = {}

                    def fn(stg, st_tok, sl_=sl, tk_=tk, hold=tv_holder):
                        tv = vcopy(stg, sl_, waits=(tk_, st_tok))
                        hold["tv"] = tv
                        return tv
                    stage_store(fn, wt_scr[wi][fc, oc], bf=False)
                    pt_last[fc % 4] = tv_holder["tv"]
                pw_tok[b] = (ts, p.cnt[ts])

        def prep_quant_weight(w_dram, n_oc, in_dim, dst_fn, scale_sb):
            stage_q = [None]
            for oc in range(n_oc):
                b = oc % 2
                ld = p.dma(S, ld_p[b], pw[b][:, 0:in_dim],
                           w_dram[oc * 128:oc * 128 + 128, :],
                           waits=(pw_tok[b],))
                rm = vred(ssm[5], pw[b][:, 0:in_dim], OP.max, absv=True,
                          waits=(ld,))
                p.drain(V)
                vts(scale_sb[:, oc:oc + 1], ssm[5][:, 0:1], 1.0 / 127.0, None,
                    OP.mult)
                vrecip(ssm[6], ssm[5])
                p.drain(V)
                t3 = vts(ssm[7], ssm[6][:, 0:1], 127.0, None, OP.mult)
                for seg in range(0, in_dim, 512):
                    wdt = min(512, in_dim - seg)
                    tq = quant_chain(s1k[3][:, 0:wdt], pw[b][:, seg:seg + wdt],
                                     ssm[7][:, 0:1], s1k[4][:, 0:wdt],
                                     s1k[5][:, 0:wdt], s1k[2][:, 0:wdt],
                                     i32t[:, 0:wdt], waits=(t3, stage_q[0]))
                    for k in range(wdt // 128):
                        fc = (seg + k * 128) // 128
                        slot = fc % 4
                        sl = tslot(slot)
                        tk = tr_(sl, s1k[3][:, k * 128:k * 128 + 128],
                                 waits=(tq, pt_last[slot],
                                        pt_last[(slot + 2) % 4], tk_i))
                        hold = {}

                        def fn(stg, st_tok, sl_=sl, tk_=tk, h=hold):
                            tv = vcopy(stg, sl_, waits=(tk_, st_tok))
                            h["tv"] = tv
                            return tv
                        stage_store(fn, dst_fn(fc, oc), bf=True)
                        pt_last[slot] = hold["tv"]
                    stage_q[0] = (ts, p.cnt[ts])
                pw_tok[b] = (vs, p.cnt[vs])

        prep_quant_weight(wd_d, FC, DFF, lambda fc, oc: wdq_scr[fc, oc], sd_sb)

        prep_done2 = ((st_p[0], p.cnt[st_p[0]]), (st_p[1], p.cnt[st_p[1]]))
        p.barrier(bar)

        fpw_tok = [None]

        def fprep_gen():
            for w_dram, dst_fn, scale_sb in (
                    (wg_d, lambda fc, oc: wgq_scr[fc, oc], sg_sb),
                    (wu_d, lambda fc, oc: wuq_scr[fc, oc], su_sb)):
                stage_q = [None]
                for oc in range(IC):
                    ld = p.dma(S, ld_p[0], fpw[:, 0:D],
                               w_dram[oc * 128:oc * 128 + 128, :],
                               waits=(fpw_tok[0],))
                    vred(ssm[5], fpw[:, 0:D], OP.max, absv=True, waits=(ld,))
                    p.drain(V)
                    vts(scale_sb[:, oc:oc + 1], ssm[5][:, 0:1], 1.0 / 127.0,
                        None, OP.mult)
                    vrecip(ssm[6], ssm[5])
                    p.drain(V)
                    t3 = vts(ssm[7], ssm[6][:, 0:1], 127.0, None, OP.mult)
                    for seg in range(0, D, 512):
                        wdt = min(512, D - seg)
                        tq = quant_chain(s512[3][:, 0:wdt],
                                         fpw[:, seg:seg + wdt],
                                         ssm[7][:, 0:1], s512[0][:, 0:wdt],
                                         s512[1][:, 0:wdt], s512[2][:, 0:wdt],
                                         i32t[:, 0:wdt],
                                         waits=(t3, stage_q[0]))
                        for k in range(wdt // 128):
                            fc = (seg + k * 128) // 128
                            slot = fc % 4
                            sl = tslot(slot)
                            tk = tr_(sl, s512[3][:, k * 128:k * 128 + 128],
                                     waits=(tq, pt_last[slot],
                                            pt_last[(slot + 2) % 4], tk_i))
                            hold = {}

                            def fn(stg, st_tok_, sl_=sl, tk_=tk, h=hold):
                                tv = vcopy(stg, sl_, waits=(tk_, st_tok_))
                                h["tv"] = tv
                                return tv
                            stage_store(fn, dst_fn(fc, oc), bf=True)
                            pt_last[slot] = hold["tv"]
                        stage_q[0] = (ts, p.cnt[ts])
                    fpw_tok[0] = (vs, p.cnt[vs])
                    yield

        fgen = fprep_gen()

        for s_i in range(NSEQ):
            xt_bufs = [s1k[0], s1k[1]]
            xt_tok = [None, None]
            for tt in range(TT):
                b = tt % 2
                ld = p.dma(S, ld_x[b], xt_bufs[b][:, 0:D],
                           x_d[s_i, tt * 128:tt * 128 + 128, :],
                           waits=(xt_tok[b],))
                for fc in range(FC):
                    transpose_to(fc % 4, xT[:, fc, tt * 128:tt * 128 + 128],
                                 xt_bufs[b][:, fc * 128:fc * 128 + 128],
                                 waits=(ld,))
                xt_tok[b] = (ts, p.cnt[ts])
                next(fgen, None)
            xT_ready = (vs, p.cnt[vs])

            ws_tok = [None, None]
            head_done = [None]
            for hp in range(NHP):
                qk_evacs = []
                for pi, name in ((0, "q"), (1, "k"), (2, "v")):
                    b = (hp * 3 + pi) % 2
                    wld = p.dma(S, ld_w[b], ws[:, b, 0:FC, :],
                                wt_scr[pi][:, hp].rearrange("a p c -> p a c"),
                                waits=(ws_tok[b],) + prep_done2)
                    if name in ("q", "k"):
                        dst = qt if name == "q" else kt
                        for st_ in range(SL // OSW):
                            pb = st_ % 2
                            tk = None
                            for fc in range(FC):
                                tk = mm(p_big[pb][:, 0:OSW], ws[:, b, fc, :],
                                        xT[:, fc, st_ * OSW:st_ * OSW + OSW],
                                        start=(fc == 0), stop=(fc == FC - 1),
                                        waits=(wld, xT_ready, pb_last[pb])
                                        if fc == 0 else ())
                            vcopy(dst[0:64, st_ * OSW:st_ * OSW + OSW],
                                  p_big[pb][0:64, 0:OSW], waits=(tk,))
                            tv1 = vcopy(
                                dst[64:128, SL + st_ * OSW:SL + st_ * OSW + OSW],
                                p_big[pb][64:128, 0:OSW], waits=())
                            pb_last[pb] = tv1
                            qk_evacs.append(tv1)
                        ws_tok[b] = (ts, p.cnt[ts])
                    else:
                        for gt in range(TT // 4):
                            pb = gt % 2
                            tk = None
                            for t4 in range(4):
                                tt = gt * 4 + t4
                                for fc in range(FC):
                                    tk = mm(p_big[pb][:, t4 * 128:t4 * 128 + 128],
                                            xT[:, fc, tt * 128:tt * 128 + 128],
                                            ws[:, b, fc, :],
                                            start=(t4 == 0 and fc == 0),
                                            stop=(t4 == 3 and fc == FC - 1),
                                            waits=(wld, xT_ready, pb_last[pb])
                                            if (t4 == 0 and fc == 0) else (),
                                            skip=True)
                            tvl = None
                            for t4 in range(4):
                                tt = gt * 4 + t4
                                vcopy(v65[:, tt, 0:64],
                                      p_big[pb][:, t4 * 128:t4 * 128 + 64],
                                      waits=(tk,))
                                tvl = vcopy(v65[:, tt, 65:129],
                                            p_big[pb][:, t4 * 128 + 64:
                                                       t4 * 128 + 128])
                            pb_last[pb] = tvl
                            qk_evacs.append(tvl)
                        ws_tok[b] = (ts, p.cnt[ts])
                qk_ready = (vs, p.cnt[vs])

                for h in range(2):
                    g = 2 * hp + h
                    col0 = h * SL
                    prev_head = head_done[0]
                    pv_tok = {}
                    for kc in range(KC):
                        eb = kc % 2
                        q0 = kc * 128
                        exps = []
                        while q0 < SL:
                            wdt = min(512, SL - q0)
                            pb = len(exps) % 2
                            w0 = (qk_ready, pb_last[pb], pv_tok.get(kc - 2))
                            tkm = mm(p_big[pb][:, 0:wdt],
                                     kt[:, col0 + kc * 128:col0 + kc * 128 + 128],
                                     qt[:, col0 + q0:col0 + q0 + wdt],
                                     start=True, stop=True, waits=w0)
                            if q0 == kc * 128:
                                tkm = vtt(p_big[pb][:, 0:128], p_big[pb][:, 0:128],
                                          mask[:], OP.add, waits=(tkm, tk_m))
                            tke = act(e_sb[eb][:, q0 - kc * 128:
                                               q0 - kc * 128 + wdt],
                                      p_big[pb][:, 0:wdt], AF.Exp,
                                      bias=consts[:, CB_EXP:CB_EXP + 1],
                                      scale=0.125, waits=(tkm,))
                            pb_last[pb] = tke
                            exps.append(tke)
                            q0 += wdt
                        tkpv = None
                        for qi in range(kc, KC):
                            w0 = tuple(exps) if qi == kc else ()
                            if qi == kc and kc == 0 and prev_head is not None:
                                w0 = w0 + (prev_head,)
                            tkpv = mm(p_pv[:, qi * 128:qi * 128 + 65],
                                      e_sb[eb][:, (qi - kc) * 128:
                                               (qi - kc) * 128 + 128],
                                      v65[:, kc, h * 65:h * 65 + 65],
                                      start=(kc == 0 and qi % 4 == 0),
                                      stop=(qi == kc), waits=w0, skip=True)
                        pv_tok[kc] = tkpv
                    for qi in range(KC):
                        ab = qi % 2
                        p.drain(V)
                        vrecip(ssm[0], p_pv[:, qi * 128 + 64:qi * 128 + 65],
                               waits=(pv_tok[KC - 1], stg_tok[ab]))
                        p.drain(V)
                        tv = vts(astg[ab][:], p_pv[:, qi * 128:qi * 128 + 64],
                                 ssm[0][:, 0:1], None, OP.mult)
                        td = p.dma(G, st_at[ab], attn_scr[g, qi], astg[ab][:],
                                   waits=(tv,))
                        stg_tok[ab] = td
                    head_done[0] = (vs, p.cnt[vs])
                    next(fgen, None)
            attn_done2 = ((st_at[0], p.cnt[st_at[0]]),
                          (st_at[1], p.cnt[st_at[1]]))
            p.barrier(bar)

            wo_ld = None
            for fc in range(FC):
                wo_ld = p.dma(S, ld_w[0], wo_view[:, fc, 0:D],
                              wt_scr[3][fc].rearrange("c p d -> p c d"),
                              waits=prep_done2 + (ws_tok[0], ws_tok[1])
                              if fc == 0 else ())
            asm_bufs = [a2gap[0], a2gap[1]]
            att_bufs = [a2gap[2], a2gap[3]]
            xt2_bufs = [a2gap[4], a2gap[5]]
            r1pre, tcb, sqb, r1out = s1k[2], s1k[3], s1k[4], s1k[5]
            btok = {}
            prev_ln = None
            for tt in range(TT):
                b = tt % 2
                lda = p.dma(S, ld_at[b], asm_bufs[b][:, 0:NH * 64],
                            attn_scr[:, tt].rearrange("a p c -> p a c"),
                            waits=attn_done2 + (btok.get(("asm", b)),))
                ldx = p.dma(S, ld_x[b], xt2_bufs[b][:, 0:D],
                            x_d[s_i, tt * 128:tt * 128 + 128, :],
                            waits=(btok.get(("xt2", b)),))
                trv = None
                for fc in range(FC):
                    trv = transpose_to(fc % 4,
                                       att_bufs[b][:, fc * 128:fc * 128 + 128],
                                       asm_bufs[b][:, fc * 128:fc * 128 + 128],
                                       waits=(lda, btok.get(("att", b))))
                ot = []
                for os_ in range(D // OSW):
                    pb = os_ % 2
                    tk = None
                    for fc in range(FC):
                        tk = mm(p_big[pb][:, 0:OSW],
                                att_bufs[b][:, fc * 128:fc * 128 + 128],
                                wo_view[:, fc, os_ * OSW:os_ * OSW + OSW],
                                start=(fc == 0), stop=(fc == FC - 1),
                                waits=(trv, wo_ld, pb_last[pb])
                                if fc == 0 else ())
                    tv = vtt(r1pre[:, os_ * OSW:os_ * OSW + OSW],
                             p_big[pb][:, 0:OSW],
                             xt2_bufs[b][:, os_ * OSW:os_ * OSW + OSW],
                             OP.add, waits=(tk, ldx, btok.get(("r1p",))))
                    pb_last[pb] = tv
                    ot.append(tv)
                btok[("att", b)] = (ts, p.cnt[ts])
                tln, tA = ln_chain(r1pre[:, 0:D], r1out[:, 0:D], tcb, sqb,
                                   waits=tuple(ot) + (btok.get(("r1o",)),),
                                   prev=prev_ln)
                prev_ln = tln
                btok[("r1p",)] = tA
                tvm = vred(rmax_all[:, s_i * TT + tt:s_i * TT + tt + 1],
                           r1out[:, 0:D], OP.max, absv=True, waits=(tln,))
                td = p.dma(G, st_r1, r1_scr[s_i * TT + tt], r1out[:, 0:D],
                           waits=(tvm,))
                btok[("r1o",)] = td
                btok[("asm", b)] = (ts, p.cnt[ts])
                btok[("xt2", b)] = (vs, p.cnt[vs])
            p.barrier(bar)

        for _ in fgen:
            pass
        prep_done3 = ((st_p[0], p.cnt[st_p[0]]), (st_p[1], p.cnt[st_p[1]]))
        r1_done = (st_r1, p.cnt[st_r1])

        def global_max_and_scales(src_cols, ncols, sig_tok, sig_sb, cc_i, cc_o,
                                  g_sb, bc_a, bc_b, scls):
            p.drain(V)
            t0 = vred(ssm[0], src_cols[:, 0:ncols], OP.max)
            tt0 = tr_(p_misc[0:1, 0:128], ssm[0][:, 0:1],
                      waits=(t0, tk_i, pm_last[0]))
            t1 = vred(lmax[:], p_misc[0:1, 0:128], OP.max, waits=(tt0,))
            td1 = p.dma(G, ccd, cc_i, lmax[:], waits=(t1,))
            tcc = p.op(G, lambda e: e.collective_compute(
                "AllReduce", OP.max, replica_groups=[list(range(cfg.NC))],
                ins=[cc_i.opt()], outs=[cc_o.opt()]), waits=(td1,), inc=ccch)
            td2 = p.dma(G, ccd, g_sb[:], cc_o, waits=(tcc,))
            vtt(tmp11[0][:], sig_sb[:], g_sb[:], OP.mult, waits=(td2, sig_tok))
            p.drain(V)
            ts1 = vts(tmp11[1][:], tmp11[0][:], 1.0 / 127.0, None, OP.mult)
            vrecip(tmp11[2][:], tmp11[0][:])
            p.drain(V)
            ts2 = vts(tmp11[3][:], tmp11[2][:], 127.0, None, OP.mult)
            tb1 = mm(p_misc[:, 128:129], ones_row[:], tmp11[1][:], True, True,
                     waits=(ts1, tk_o, t1))
            tv1 = vcopy(bc_a[:], p_misc[:, 128:129], waits=(tb1,))
            tb2 = mm(p_misc[:, 256:257], ones_row[:], tmp11[3][:], True, True,
                     waits=(ts2, tv1))
            tv2 = vcopy(bc_b[:], p_misc[:, 256:257], waits=(tb2,))
            pm_last[0] = tv2
            p.drain(V)
            for dst, src in scls:
                vts(dst, src, bc_a[:, 0:1], None, OP.mult, waits=(tv1, tv2))
            p.drain(V)

        global_max_and_scales(rmax_all, NT16, tk_sig1, sig_cx, cc_io[0],
                              cc_io[1], gmax[0], sx_bc, rsx_bc,
                              [(gsc_sb[:, 0:IC], sg_sb[:, 0:IC]),
                               (usc_sb[:, 0:IC], su_sb[:, 0:IC])])
        p.barrier(bar)

        hcol = [0]
        r1b = [s1k[0], s1k[1]]
        rb_tok = [None, None]
        wsl_tok = [None, None]
        st_tok = {}
        xq_last_mm = [None] * NXQ
        xq_done = [None, None]
        xq_ready_buf = [None] * NXQ

        def f1_stage(blk):
            b2 = blk % 2
            bq = blk % NXQ
            xcol = b2 * FBLK
            ntt = FBLK // 128
            for k in range(ntt):
                b = (blk * ntt + k) % 2
                ld = p.dma(S, ld_r1[b], r1b[b][:, 0:D], r1_scr[blk * ntt + k],
                           waits=(r1_done, rb_tok[b]))
                for fc in range(FC):
                    transpose_to(fc % 4,
                                 xT[:, fc, xcol + k * 128:xcol + k * 128 + 128],
                                 r1b[b][:, fc * 128:fc * 128 + 128],
                                 waits=(ld, xq_done[b2]))
                rb_tok[b] = (ts, p.cnt[ts])
            xTf_ready = (vs, p.cnt[vs])
            tq = None
            for fc in range(FC):
                tq = quant_chain(xqT[bq][:, fc, 0:FBLK],
                                 xT[:, fc, xcol:xcol + FBLK],
                                 rsx_bc[:, 0:1], s1k[2][:, 0:FBLK],
                                 s1k[3][:, 0:FBLK], s1k[4][:, 0:FBLK],
                                 i32t[:, 0:FBLK],
                                 waits=(xTf_ready, xq_last_mm[bq])
                                 + prep_done3)
            xq_done[b2] = tq
            xq_ready_buf[bq] = tq

        ppv_last = {0: None, 1: None}
        for blk in range(min(2, NBLK)):
            f1_stage(blk)
        bp = 0
        while 2 * bp < NBLK:
            blocks = [2 * bp] + ([2 * bp + 1] if 2 * bp + 1 < NBLK else [])
            if 2 * bp + 2 < NBLK and NXQ > 2:
                f1_stage(2 * bp + 2)
            for ic in range(IC):
                b = ic % 2
                wld = p.dma(S, ld_w[b], wslg[:, b, 0:FC, :],
                            wgq_scr[:, ic].rearrange("a p c -> p a c"),
                            waits=(wsl_tok[b],) + prep_done3)
                wld = p.dma(S, ld_w[b], wslg[:, b, FC:2 * FC, :],
                            wuq_scr[:, ic].rearrange("a p c -> p a c"))
                for j, blk in enumerate(blocks):
                    bq = blk % NXQ
                    pg = p_big[0][:, 0:FBLK] if j == 0 else p_pv[:, 0:FBLK]
                    pu = (p_big[1][:, 0:FBLK] if j == 0
                          else p_pv[:, FBLK:2 * FBLK])
                    gl = pb_last[0] if j == 0 else ppv_last[0]
                    ul = pb_last[1] if j == 0 else ppv_last[1]
                    tkg = tku = None
                    for fc in range(FC):
                        tkg = mm(pg, wslg[:, b, fc, :],
                                 xqT[bq][:, fc, 0:FBLK],
                                 start=(fc == 0), stop=(fc == FC - 1),
                                 waits=(wld, xq_ready_buf[bq], gl)
                                 if fc == 0 else ())
                    for fc in range(FC):
                        tku = mm(pu, wslg[:, b, FC + fc, :],
                                 xqT[bq][:, fc, 0:FBLK],
                                 start=(fc == 0), stop=(fc == FC - 1),
                                 waits=(ul,) if fc == 0 else ())
                    tas = act(s512[j][:, 0:FBLK], pg, AF.Silu,
                              scale=gsc_sb[:, ic:ic + 1],
                              waits=(tkg, st_tok.get(("h", j))))
                    tvu = vts(s512[2 + j][:, 0:FBLK], pu,
                              usc_sb[:, ic:ic + 1], None, OP.mult,
                              waits=(tku, st_tok.get(("h", j))))
                    if j == 0:
                        pb_last[0] = tas
                        pb_last[1] = tvu
                    else:
                        ppv_last[0] = tas
                        ppv_last[1] = tvu
                    hst = s512[4 + j]
                    tvh = vtt(hst[:, 0:FBLK], s512[j][:, 0:FBLK],
                              s512[2 + j][:, 0:FBLK],
                              OP.mult, waits=(tas, tvu,
                                              st_tok.get(("hst", j))))
                    st_tok[("h", j)] = tvh
                    tvm = vred(hmax_all[:, hcol[0]:hcol[0] + 1],
                               hst[:, 0:FBLK],
                               OP.max, absv=True, waits=(tvh,))
                    hcol[0] += 1
                    tdh = p.dma(G, st_h[j],
                                h_scr[ic][:, blk * FBLK:blk * FBLK + FBLK],
                                hst[:, 0:FBLK], waits=(tvm,))
                    st_tok[("hst", j)] = tdh
                wsl_tok[b] = (ts, p.cnt[ts])
            for blk in blocks:
                xq_last_mm[blk % NXQ] = (ts, p.cnt[ts])
            if 2 * bp + 3 < NBLK and NXQ > 2:
                f1_stage(2 * bp + 3)
            elif 2 * bp + 2 < NBLK and NXQ == 2:
                f1_stage(2 * bp + 2)
                if 2 * bp + 3 < NBLK:
                    f1_stage(2 * bp + 3)
            bp += 1
        p.barrier(bar)
        h_done2 = ((st_h[0], p.cnt[st_h[0]]), (st_h[1], p.cnt[st_h[1]]))

        global_max_and_scales(hmax_all, hcol[0], tk_sig2, sig_cd, cc_io[2],
                              cc_io[3], gmax[1], s2_bc, rq2_bc,
                              [(dsc_sb[:, 0:FC], sd_sb[:, 0:FC])])
        p.barrier(bar)

        NTK = FSP // 128
        hlb = [s1k[0], s1k[1]]
        hb_tok = [None, None]
        wdl_tok = [None, None]
        dstg = [s512[4], s512[5]]
        dstg_tok = [None, None]
        hq_last_mm = [None, None]
        rb_tok = {}
        prev_ln2 = None
        r1b = [f2v[5], f2v[6]]
        yb = [f2v[0], f2v[1]]
        r2pre, tcb, sqb = f2v[2], f2v[3], f2v[4]
        hq_ready_buf = [None] * NHQ
        hq_last_mm = [None] * NHQ
        f2_pair = [0]

        def f2_stage(sp):
            bq = sp % NHQ
            tq = None
            ic = 0
            while ic < IC:
                npair = min(max(512 // FSP, 1), IC - ic)
                wdt = FSP * npair
                b = f2_pair[0] % 2
                f2_pair[0] += 1
                ld = None
                for j in range(npair):
                    ld = p.dma(S, ld_h[b], hlb[b][:, j * FSP:j * FSP + FSP],
                               h_scr[ic + j][:, sp * FSP:sp * FSP + FSP],
                               waits=h_done2 + (hb_tok[b],) if j == 0 else ())
                tq = quant_chain(
                    hqT[bq][:].rearrange("p a b -> p (a b)")[:, ic * FSP:
                                                             ic * FSP + wdt],
                    hlb[b][:, 0:wdt], rq2_bc[:, 0:1],
                    s1k[2][:, 0:wdt], s1k[3][:, 0:wdt], s1k[4][:, 0:wdt],
                    i32t[:, 0:wdt], waits=(ld, hq_last_mm[bq]))
                hb_tok[b] = tq
                ic += npair
            hq_ready_buf[bq] = tq

        for sp in range(min(2, NSP)):
            f2_stage(sp)
        bp = 0
        while 2 * bp < NSP:
            strips = [2 * bp] + ([2 * bp + 1] if 2 * bp + 1 < NSP else [])
            if 2 * bp + 2 < NSP and NHQ > 2:
                f2_stage(2 * bp + 2)
            for oc in range(FC):
                b = oc % 2
                wl = p.dma(S, ld_w[b], wsld[:, b, 0:IC, :],
                           wdq_scr[:, oc].rearrange("a p c -> p a c"),
                           waits=(wdl_tok[b],) + prep_done3)
                for j, sp in enumerate(strips):
                    bq = sp % NHQ
                    tk = None
                    for ic in range(IC):
                        tk = mm(p_big[j][:, 0:FSP], wsld[:, b, ic, :],
                                hqT[bq][:, ic, 0:FSP], start=(ic == 0),
                                stop=(ic == IC - 1),
                                waits=(wl, hq_ready_buf[bq], pb_last[j])
                                if ic == 0 else ())
                    tvd = vts(dstg[j][:, 0:FSP], p_big[j][:, 0:FSP],
                              dsc_sb[:, oc:oc + 1], None, OP.mult,
                              waits=(tk, dstg_tok[j]))
                    pb_last[j] = tvd
                    for k in range(NTK):
                        kk = j * NTK + k
                        transpose_to((oc * 2 * NTK + kk) % 4,
                                     outpre[:, kk, oc * 128:oc * 128 + 128],
                                     dstg[j][:, k * 128:k * 128 + 128],
                                     waits=(tvd, rb_tok.get(("opre", kk))))
                    dstg_tok[j] = (ts, p.cnt[ts])
                wdl_tok[b] = (ts, p.cnt[ts])
            for sp in strips:
                hq_last_mm[sp % NHQ] = (ts, p.cnt[ts])
            if 2 * bp + 3 < NSP and NHQ > 2:
                f2_stage(2 * bp + 3)
            elif 2 * bp + 2 < NSP and NHQ == 2:
                f2_stage(2 * bp + 2)
                if 2 * bp + 3 < NSP:
                    f2_stage(2 * bp + 3)
            outpre_ready = (vs, p.cnt[vs])
            for k in range(len(strips) * NTK):
                b = k % 2
                gi = 2 * bp * NTK + k
                ld = p.dma(S, ld_r1[b], r1b[b][:, 0:D], r1_scr[gi],
                           waits=(r1_done, rb_tok.get(("rb", b))))
                tv = vtt(r2pre[:, 0:D], outpre[:, k, 0:D], r1b[b][:, 0:D],
                         OP.add, waits=(ld, outpre_ready, rb_tok.get(("r2",))))
                rb_tok[("opre", k)] = tv
                tln, tA = ln_chain(r2pre[:, 0:D], yb[b][:, 0:D], tcb, sqb,
                                   waits=(tv, rb_tok.get(("yb", b))),
                                   prev=prev_ln2)
                prev_ln2 = tln
                rb_tok[("r2",)] = tA
                s_idx = gi // TT
                t_idx = gi % TT
                td = p.dma(G, st_y[b], y_d[s_idx, t_idx * 128:t_idx * 128 + 128, :],
                           yb[b][:, 0:D], waits=(tln,))
                rb_tok[("yb", b)] = td
                rb_tok[("rb", b)] = tln
            bp += 1
        p.barrier(bar)

        p.op(G, lambda e: e.nop(),
             waits=((st_y[0], p.cnt[st_y[0]]), (st_y[1], p.cnt[st_y[1]])))

        _safe = SAFE_EVERY
        if _safe > 0:
            newops = []
            cnt2 = 0
            bcnt = {e: 0 for e in ENGINES}
            for rec in p.ops:
                newops.append(rec)
                if rec[3] is not None and rec[3][0].startswith("bar_"):
                    bcnt[rec[3][0][4:]] += 1
                cnt2 += 1
                if cnt2 % _safe == 0:
                    vals = {}
                    for e in ENGINES:
                        bcnt[e] += 1
                        vals[e] = bcnt[e]
                        newops.append((e, lambda eng: eng.nop(), (), (bar[e], 1)))
                    for e in ENGINES:
                        w = tuple((bar[o], vals[o]) for o in ENGINES if o != e)
                        newops.append((e, lambda eng: eng.nop(), w, None))
            p.ops = newops
            for e in ENGINES:
                p.cnt[bar[e]] = bcnt[e]
        p.compress((ts, vs, as_))
        p.check()
        _LAST_PLAN[:] = [p]

        with nc.Block() as block:
            def make(eng_name):
                def body(eng):
                    waited = {}
                    for rec in p.ops:
                        if rec[0] != eng_name:
                            continue
                        for s_, v_ in rec[2]:
                            if waited.get(s_, 0) >= v_:
                                continue
                            eng.wait_ge(sems[s_], v_)
                            waited[s_] = v_
                        inst = rec[1](eng)
                        if rec[3] is not None:
                            inst.then_inc(sems[rec[3][0]], rec[3][1])
                return body
            for e_name in ENGINES:
                getattr(block, e_name)(make(e_name))

    return nc


def host_inputs(cfg, x_shard, Wq, Wk, Wv, Wo, clip_x, clip_down,
                w_gate, w_up, w_down):
    consts = np.zeros((128, 8), np.float32)
    consts[:, 0] = -4.0
    consts[:, 1] = 1e-5
    consts[:, 2] = 0.5
    consts[:, 3] = 1.0
    mask = np.where(np.arange(128)[:, None] <= np.arange(128)[None, :],
                    np.float32(0.0), np.float32(-1e9))
    return {
        "x": np.ascontiguousarray(x_shard, dtype=np.float32),
        "wq": np.ascontiguousarray(Wq, dtype=np.float32),
        "wk": np.ascontiguousarray(Wk, dtype=np.float32),
        "wv": np.ascontiguousarray(Wv, dtype=np.float32),
        "wo": np.ascontiguousarray(Wo, dtype=np.float32),
        "wg": np.ascontiguousarray(w_gate, dtype=np.float32),
        "wu": np.ascontiguousarray(w_up, dtype=np.float32),
        "wd": np.ascontiguousarray(w_down, dtype=np.float32),
        "clipx": np.asarray(clip_x, np.float32).reshape(1, 1),
        "clipd": np.asarray(clip_down, np.float32).reshape(1, 1),
        "ident": np.eye(128, dtype=np.float32),
        "mask": np.ascontiguousarray(mask),
        "consts": consts,
        "ones_row": np.ones((1, 128), np.float32),
    }


def kernel(L0_attn_x, L0_attn_Wq, L0_attn_Wk, L0_attn_Wv, L0_attn_Wo,
           L0_ffn_clip_x, L0_ffn_clip_down, L0_ffn_w_gate, L0_ffn_w_up,
           L0_ffn_w_down):
    cfg = Cfg()
    x = np.asarray(L0_attn_x, np.float32)
    per = x.shape[0] // cfg.NC
    nc = build_nc(cfg)
    in_maps = [host_inputs(cfg, x[c * per:(c + 1) * per], L0_attn_Wq,
                           L0_attn_Wk, L0_attn_Wv, L0_attn_Wo, L0_ffn_clip_x,
                           L0_ffn_clip_down, L0_ffn_w_gate, L0_ffn_w_up,
                           L0_ffn_w_down)
               for c in range(cfg.NC)]
    res = run_bass_kernel_spmd(nc, in_maps, core_ids=list(range(cfg.NC)))
    out = np.concatenate(
        [np.asarray(r["y"], np.float32).reshape(per, cfg.S, cfg.D)
         for r in res.results], axis=0)
    return out


# version 14
# speedup vs baseline: 1.0086x; 1.0086x over previous
import contextlib
import numpy as np

import concourse.bass as bass
import concourse.mybir as mybir
from concourse.bass_utils import run_bass_kernel_spmd

F32 = mybir.dt.float32
BF16 = mybir.dt.float16
AF = mybir.ActivationFunctionType
OP = mybir.AluOpType
AXX = mybir.AxisListType.X

ENGINES = ("sync", "tensor", "vector", "scalar", "gpsimd")
SAFE_EVERY = 0
_LAST_PLAN = []
T, V, A, S, G = "tensor", "vector", "scalar", "sync", "gpsimd"


class Cfg:
    def __init__(self, nseq=4, s=2048, d=1024, dff=2816, ncores=8):
        self.NSEQ = nseq
        self.S = s
        self.D = d
        self.DFF = dff
        self.NC = ncores
        self.FC = d // 128
        self.IC = dff // 128
        self.TT = s // 128
        self.KC = s // 128
        self.NH = d // 64
        self.TOK = nseq * s
        self.NT16 = self.TOK // 128
        self.FBLK = min(512, self.TOK)
        self.NBLK = self.TOK // self.FBLK
        self.FSP = 256 if self.TOK >= 512 else self.TOK
        self.NSP = self.TOK // self.FSP


class Plan:

    def __init__(self):
        self.ops = []
        self.cnt = {}
        self.sem_names = []
        self.eng_sem = {}
        self.pending = {}

    def sem(self, name):
        self.sem_names.append(name)
        self.cnt[name] = 0
        return name

    def op(self, eng, fn, waits=(), inc=None, amt=1):
        waits = tuple(w for w in waits if w is not None)
        pend = self.pending.pop(eng, None)
        if pend is not None:
            waits = waits + (pend,)
        tok = None
        if inc is not None:
            self.cnt[inc] += amt
            tok = (inc, self.cnt[inc])
        self.ops.append((eng, fn, waits, (inc, amt) if inc else None))
        return tok

    def dma(self, eng, ch, out, in_, waits=()):
        return self.op(eng, lambda e, o=out, i=in_: e.dma_start(out=o, in_=i),
                       waits=waits, inc=ch, amt=16)

    def drain(self, eng):
        sem = self.eng_sem.get(eng)
        if sem is None or self.cnt[sem] == 0:
            self.op(eng, lambda e: e.drain())
            return
        self.pending[eng] = (sem, self.cnt[sem])

    def barrier(self, bar):
        vals = {}
        for e in ENGINES:
            self.cnt[bar[e]] += 1
            vals[e] = self.cnt[bar[e]]
            self.ops.append((e, lambda eng: eng.nop(), (), (bar[e], 1)))
        for e in ENGINES:
            waits = tuple((bar[o], vals[o]) for o in ENGINES if o != e)
            self.ops.append((e, lambda eng: eng.nop(), waits, None))

    def compress(self, sems_to_strip):
        waited = {}
        for _, _, waits, _ in self.ops:
            for s_, v_ in waits:
                waited.setdefault(s_, set()).add(v_)
        cnt = {s_: 0 for s_ in self.cnt}
        kept = {s_: 0 for s_ in self.cnt}
        remap = {}
        newops = []
        for eng, fn, waits, inc in self.ops:
            w2 = tuple((s_, remap.get((s_, v_), v_)) for s_, v_ in waits)
            inc2 = inc
            if inc is not None:
                s_, amt = inc
                cnt[s_] += amt
                if s_ in sems_to_strip:
                    if cnt[s_] in waited.get(s_, ()):
                        kept[s_] += amt
                        inc2 = (s_, amt)
                    else:
                        inc2 = None
                    remap[(s_, cnt[s_])] = kept[s_]
                else:
                    kept[s_] += amt
            newops.append((eng, fn, w2, inc2))
        self.ops = newops

    def check(self):
        q = {e: [i for i, rec in enumerate(self.ops) if rec[0] == e] for e in ENGINES}
        pos = {e: 0 for e in ENGINES}
        sem = {s_: 0 for s_ in self.cnt}
        progress = True
        while progress:
            progress = False
            for e in ENGINES:
                while pos[e] < len(q[e]):
                    rec = self.ops[q[e][pos[e]]]
                    if any(sem[s_] < v for s_, v in rec[2]):
                        break
                    if rec[3]:
                        sem[rec[3][0]] += rec[3][1]
                    pos[e] += 1
                    progress = True
        stuck = {e: len(q[e]) - pos[e] for e in ENGINES if pos[e] < len(q[e])}
        if stuck:
            details = []
            for e in stuck:
                rec = self.ops[q[e][pos[e]]]
                missing = [(s_, v, sem[s_]) for s_, v in rec[2] if sem[s_] < v]
                details.append(f"{e}@{pos[e]}/{len(q[e])} waiting {missing}")
            raise RuntimeError("plan deadlock: " + "; ".join(details))


def build_nc(cfg: Cfg):
    nc = bass.Bass("TRN2", target_bir_lowering=False)
    NSEQ, SL, D, DFF = cfg.NSEQ, cfg.S, cfg.D, cfg.DFF
    FC, IC, TT, KC, NH = cfg.FC, cfg.IC, cfg.TT, cfg.KC, cfg.NH
    TOK, NT16 = cfg.TOK, cfg.NT16
    FBLK, NBLK, NSP, FSP = cfg.FBLK, cfg.NBLK, cfg.NSP, cfg.FSP
    NHP = NH // 2
    OSW = min(512, D)

    x_d = nc.dram_tensor("x", [NSEQ, SL, D], F32, kind="ExternalInput").ap()
    w_attn_d = [nc.dram_tensor(n, [D, D], F32, kind="ExternalInput").ap()
                for n in ("wq", "wk", "wv", "wo")]
    wg_d = nc.dram_tensor("wg", [DFF, D], F32, kind="ExternalInput").ap()
    wu_d = nc.dram_tensor("wu", [DFF, D], F32, kind="ExternalInput").ap()
    wd_d = nc.dram_tensor("wd", [D, DFF], F32, kind="ExternalInput").ap()
    clipx_d = nc.dram_tensor("clipx", [1, 1], F32, kind="ExternalInput").ap()
    clipd_d = nc.dram_tensor("clipd", [1, 1], F32, kind="ExternalInput").ap()
    ident_d = nc.dram_tensor("ident", [128, 128], F32, kind="ExternalInput").ap()
    mask_d = nc.dram_tensor("mask", [128, 128], F32, kind="ExternalInput").ap()
    consts_d = nc.dram_tensor("consts", [128, 8], F32, kind="ExternalInput").ap()
    ones_d = nc.dram_tensor("ones_row", [1, 128], F32, kind="ExternalInput").ap()
    y_d = nc.dram_tensor("y", [NSEQ, SL, D], F32, kind="ExternalOutput").ap()

    wt_scr = [nc.dram_tensor(f"wt{i}", [FC, FC, 128, 128], F32).ap()
              for i in range(4)]
    wgq_scr = nc.dram_tensor("wgq", [FC, IC, 128, 128], BF16).ap()
    wuq_scr = nc.dram_tensor("wuq", [FC, IC, 128, 128], BF16).ap()
    wdq_scr = nc.dram_tensor("wdq", [IC, FC, 128, 128], BF16).ap()
    attn_scr = nc.dram_tensor("attns", [NH, TT, 128, 64], F32).ap()
    r1_scr = nc.dram_tensor("r1s", [NT16, 128, D], F32).ap()
    h_scr = nc.dram_tensor("hs", [IC, 128, TOK], F32).ap()
    cc_io = [nc.dram_tensor(f"cc{i}", [1, 1], F32).ap() for i in range(4)]

    CB_EXP, CB_EPS, CB_HALF, CB_ONE, CB_ZERO = 0, 1, 2, 3, 5

    p = Plan()
    ld_c = p.sem("ld_c")
    ld_p = [p.sem(f"ld_p{i}") for i in range(2)]
    st_p = [p.sem(f"st_p{i}") for i in range(2)]
    ld_w = [p.sem(f"ld_w{i}") for i in range(2)]
    ld_x = [p.sem(f"ld_x{i}") for i in range(2)]
    ld_r1 = [p.sem(f"ld_r1{i}") for i in range(2)]
    ld_h = [p.sem(f"ld_h{i}") for i in range(2)]
    ld_at = [p.sem(f"ld_at{i}") for i in range(2)]
    st_at = [p.sem(f"st_at{i}") for i in range(2)]
    st_r1 = p.sem("st_r1")
    st_h = [p.sem(f"st_h{i}") for i in range(2)]
    st_y = [p.sem(f"st_y{i}") for i in range(2)]
    ccch = p.sem("ccch"); ccd = p.sem("ccd")
    ts = p.sem("ts"); vs = p.sem("vs"); as_ = p.sem("as_")
    p.eng_sem = {T: ts, V: vs, A: as_}
    bar = {e: p.sem("bar_" + e) for e in ENGINES}

    with contextlib.ExitStack() as stk:
        ent = stk.enter_context

        def sbt(name, shape, dt=F32):
            return ent(nc.sbuf_tensor(name, shape, dt))

        AW = max(FC * SL, 2 * DFF, 4 * D)
        arena = sbt("arena", [128, AW])
        xT = arena[:, 0:FC * SL].rearrange("p (a b) -> p a b", a=FC)
        qt = sbt("qt", [128, 2 * SL])
        kt = sbt("kt", [128, 2 * SL])
        v65 = sbt("v65", [128, KC, 130])
        e_sb = [sbt(f"e{i}", [128, SL]) for i in range(2)]
        ws = sbt("ws", [128, 2, FC, 128])
        use_gaps = (FC >= 6 and SL >= 2 * D and AW >= 11 * D)
        n1k = 6 if use_gaps else 12
        s1k = [sbt(f"s1k{i}", [128, max(1024, D)]) for i in range(n1k)]
        if use_gaps:
            a2gap = [arena[:, i * SL + D:(i + 1) * SL] for i in range(6)]
            f2v = [arena[:, (4 + i) * D:(5 + i) * D] for i in range(7)]
        else:
            a2gap = [s1k[6], s1k[7], s1k[8], s1k[9], s1k[0], s1k[1]]
            f2v = [s1k[5], s1k[6], s1k[7], s1k[8], s1k[9], s1k[10], s1k[11]]
        s512 = [sbt(f"s512{i}", [128, 512]) for i in range(6)]
        ssm = [sbt(f"ssm{i}", [128, 1]) for i in range(8)]
        rmax_all = sbt("rmax_all", [128, max(NT16, 2)])
        nhc = max(IC * (FBLK // 512) * NBLK, 2)
        hmax_all = sbt("hmax_all", [128, nhc])
        sg_sb = sbt("sg", [128, IC]); su_sb = sbt("su", [128, IC])
        sd_sb = sbt("sd", [128, FC])
        gsc_sb = sbt("gsc", [128, IC]); usc_sb = sbt("usc", [128, IC])
        dsc_sb = sbt("dsc", [128, FC])
        consts = sbt("consts_sb", [128, 8])
        ident = sbt("ident_sb", [128, 128])
        ident_bf = sbt("identbf_sb", [128, 128], BF16)
        mask = sbt("mask_sb", [128, 128])
        ones_row = sbt("ones_sb", [1, 128])
        scx = sbt("scx", [1, 1]); scd_ = sbt("scd", [1, 1])
        sig_cx = sbt("sigcx", [1, 1]); sig_cd = sbt("sigcd", [1, 1])
        lmax = sbt("lmax", [1, 1])
        gmax = [sbt(f"gmax{i}", [1, 1]) for i in range(2)]
        tmp11 = [sbt(f"t11_{i}", [1, 1]) for i in range(4)]
        sx_bc = sbt("sx_bc", [128, 1]); rsx_bc = sbt("rsx_bc", [128, 1])
        s2_bc = sbt("s2_bc", [128, 1]); rq2_bc = sbt("rq2_bc", [128, 1])
        bfw = max(2 * FC * FBLK + 4 * FC * 128,
                  2 * IC * FSP + 2 * IC * 128, 9216 + 32)
        bfa = sbt("bfa", [128, bfw], BF16)
        fpw = sbt("fpw", [128, D])
        i32t = sbt("i32t", [128, 512], mybir.dt.int32)

        NXQ = 3 if 2 * FC * FBLK + 4 * FC * 128 + FC * FBLK <= bfw else 2
        xqT = [bfa[:, i * FC * FBLK:(i + 1) * FC * FBLK].rearrange(
            "p (a b) -> p a b", a=FC) for i in range(NXQ)]
        wslg = bfa[:, NXQ * FC * FBLK:NXQ * FC * FBLK + 4 * FC * 128].rearrange(
            "p (a b c) -> p a b c", a=2, c=128)
        NHQ = 3 if 3 * IC * FSP + 2 * IC * 128 <= bfw else 2
        hqT = [bfa[:, i * IC * FSP:(i + 1) * IC * FSP].rearrange(
            "p (a b) -> p a b", a=IC) for i in range(NHQ)]
        wsld = bfa[:, NHQ * IC * FSP:NHQ * IC * FSP + 2 * IC * 128].rearrange(
            "p (a b c) -> p a b c", a=2, c=128)
        wqstg = bfa[:, 0:1024]
        qstg = bfa[:, 8192:9216] if bfw >= 9216 else bfa[:, 1024:2048]

        astg = [s512[4][:, 0:64], s512[5][:, 0:64]]
        pw = [arena[:, i * DFF:(i + 1) * DFF] for i in range(2)]
        wo_view = xT
        outpre = arena[:, 0:4 * D].rearrange("p (a b) -> p a b", a=4)

        p_big = [ent(nc.psum_tensor(f"pb{i}", [128, 512], F32)) for i in range(2)]
        p_pv = ent(nc.psum_tensor("ppv", [128, KC * 128], F32))
        p_t = ent(nc.psum_tensor("pt", [128, 512], F32))
        p_misc = ent(nc.psum_tensor("pm", [128, 512], F32))

        sems = {nm: ent(nc.semaphore(nm)) for nm in p.sem_names}

        pb_last = {0: None, 1: None}
        pt_last = {k: None for k in range(4)}
        pm_last = [None]

        def _ap(x):
            return x if isinstance(x, bass.AP) else x[:]

        def mm(out, lhsT, rhs, start, stop, waits=(), skip=False):
            out, lhsT, rhs = _ap(out), _ap(lhsT), _ap(rhs)
            return p.op(T, lambda e, o=out, l=lhsT, r=rhs, s1=start, s2=stop,
                        sk=skip: e.matmul(o, l, r, start=s1, stop=s2,
                                          skip_group_check=sk),
                        waits=waits, inc=ts)

        def tr_(out, in_, waits=()):
            out, in_ = _ap(out), _ap(in_)
            idn = ident_bf[:] if in_.dtype == BF16 else ident[:]
            return p.op(T, lambda e, o=out, i=in_, n=idn: e.transpose(o, i, n),
                        waits=waits, inc=ts)

        def vcopy(out, in_, waits=()):
            out, in_ = _ap(out), _ap(in_)
            return p.op(V, lambda e, o=out, i=in_: e.tensor_copy(out=o, in_=i),
                        waits=waits, inc=vs)

        def vts(out, in0, s1, s2=None, op0=OP.mult, op1=None, waits=()):
            out, in0 = _ap(out), _ap(in0)
            s1 = s1 if isinstance(s1, (int, float)) else _ap(s1)
            if op1 is None:
                return p.op(V, lambda e, o=out, i=in0, a=s1, c0=op0:
                            e.tensor_scalar(out=o, in0=i, scalar1=a, scalar2=None,
                                            op0=c0), waits=waits, inc=vs)
            return p.op(V, lambda e, o=out, i=in0, a=s1, b=s2, c0=op0, c1=op1:
                        e.tensor_scalar(out=o, in0=i, scalar1=a, scalar2=b,
                                        op0=c0, op1=c1), waits=waits, inc=vs)

        def vtt(out, in0, in1, op, waits=()):
            out, in0, in1 = _ap(out), _ap(in0), _ap(in1)
            return p.op(V, lambda e, o=out, a=in0, b=in1, c=op:
                        e.tensor_tensor(out=o, in0=a, in1=b, op=c),
                        waits=waits, inc=vs)

        def vred(out, in_, op=OP.max, absv=False, waits=()):
            out, in_ = _ap(out), _ap(in_)
            return p.op(V, lambda e, o=out, i=in_, c=op, av=absv:
                        e.tensor_reduce(out=o, in_=i, axis=AXX, op=c,
                                        apply_absolute_value=av),
                        waits=waits, inc=vs)

        def vrecip(out, in_, waits=()):
            out, in_ = _ap(out), _ap(in_)
            return p.op(V, lambda e, o=out, i=in_: e.reciprocal(out=o, in_=i),
                        waits=waits, inc=vs)

        def act(out, in_, fn, bias=None, scale=1.0, waits=()):
            out, in_ = _ap(out), _ap(in_)
            if bias is None:
                return p.op(A, lambda e, o=out, i=in_, f=fn, sc=scale:
                            e.activation(out=o, in_=i, func=f, scale=sc),
                            waits=waits, inc=as_)
            b_ap = _ap(bias)
            return p.op(A, lambda e, o=out, i=in_, f=fn, b=b_ap, sc=scale:
                        e.activation(out=o, in_=i, func=f, bias=b, scale=sc),
                        waits=waits, inc=as_)

        def vstt(out, in0, scalar, in1, op0, op1, waits=()):
            out, in0, in1 = _ap(out), _ap(in0), _ap(in1)
            scalar = scalar if isinstance(scalar, (int, float)) else _ap(scalar)
            return p.op(V, lambda e, o=out, a=in0, s=scalar, b=in1, c0=op0,
                        c1=op1: e.scalar_tensor_tensor(
                            out=o, in0=a, scalar=s, in1=b, op0=c0, op1=c1),
                        waits=waits, inc=vs)

        def acopy(out, in_, waits=()):
            out, in_ = _ap(out), _ap(in_)
            return p.op(A, lambda e, o=out, i=in_: e.copy(out=o, in_=i),
                        waits=waits, inc=as_)

        def vmemset(ap_, val, waits=()):
            ap_ = _ap(ap_)
            return p.op(V, lambda e, a=ap_, v=val: e.memset(a, v),
                        waits=waits, inc=vs)

        def tslot(slot):
            t_ = p_t if slot % 2 == 0 else p_misc
            off = (slot // 2) * 128
            return t_[:, off:off + 128]

        def transpose_to(slot, dst, src, waits=()):
            sl = tslot(slot)
            other = (slot + 2) % 4
            tk = tr_(sl, src, waits=tuple(waits) +
                     (pt_last[slot], pt_last[other], tk_i, tk_ibf))
            tv = vcopy(dst, sl, waits=(tk,))
            pt_last[slot] = tv
            return tv

        def quant_chain(dst_bf, src, rq_ap, tmp_a, tmp_b, tmp_c, i32_ap,
                        waits=()):
            p.drain(V)
            tu = vts(tmp_a, src, rq_ap, 0.5, OP.mult, OP.add, waits=waits)
            t1 = acopy(i32_ap, tmp_a, waits=(tu,))
            p.drain(A)
            t2 = acopy(tmp_b, i32_ap)
            td = vtt(tmp_c, tmp_a, tmp_b, OP.subtract, waits=(t2,))
            p.drain(V)
            vstt(tmp_b, tmp_c, -1.0, tmp_b, OP.is_ge, OP.add)
            p.drain(V)
            vstt(tmp_b, tmp_c, 0.0, tmp_b, OP.is_ge, OP.add)
            p.drain(V)
            vstt(tmp_b, tmp_c, 1.0, tmp_b, OP.is_ge, OP.add)
            p.drain(V)
            vts(tmp_a, tmp_b, -2.0, 127.0, OP.add, OP.min)
            p.drain(V)
            return vts(dst_bf, tmp_a, -128.0, None, OP.max)

        def ln_chain(src, y_out, tc_b, sq_b, waits=(), prev=None):
            tc_b = _ap(tc_b)[:, 0:D]
            sq_b = _ap(sq_b)[:, 0:D]
            w0 = tuple(waits) + ((prev,) if prev else ())
            p.drain(V)
            t0 = vred(ssm[0], src, OP.add, waits=w0)
            p.drain(V)
            t1 = vts(ssm[1], ssm[0], -1.0 / D, None, OP.mult)
            p.drain(V)
            tA = vts(tc_b, src, ssm[1][:, 0:1], None, OP.add, waits=w0)
            tB = act(sq_b, tc_b, AF.Square, waits=(tA,))
            t2 = vred(ssm[2], sq_b, OP.add, waits=(tB,))
            tC = act(ssm[3], ssm[2], AF.Sqrt, bias=consts[:, CB_EPS:CB_EPS + 1],
                     scale=1.0 / D, waits=(t2,) + ((prev,) if prev else ()))
            t3 = vrecip(ssm[4], ssm[3], waits=(tC,))
            p.drain(V)
            t4 = vts(y_out, tc_b, ssm[4][:, 0:1], None, OP.mult, waits=(tA,))
            return t4, tA

        p.dma(S, ld_c, consts[:], consts_d)
        p.dma(S, ld_c, ident[:], ident_d)
        p.dma(S, ld_c, mask[:], mask_d)
        p.dma(S, ld_c, ones_row[:], ones_d)
        p.dma(S, ld_c, scx[:], clipx_d)
        tk_in = p.dma(S, ld_c, scd_[:], clipd_d)
        tk_i = tk_m = tk_o = tk_in

        vmemset(qt[64:128, 0:SL], 0.0)
        vmemset(qt[0:64, SL:2 * SL], 0.0)
        vmemset(kt[64:128, 0:SL], 0.0)
        vmemset(kt[0:64, SL:2 * SL], 0.0)
        for kc in range(KC):
            vmemset(v65[:, kc, 64:65], 1.0)
            vmemset(v65[:, kc, 129:130], 1.0)
        tk_ibf = vcopy(ident_bf, ident, waits=(tk_i,))
        tk_sig1 = act(sig_cx[:], scx[:], AF.Sigmoid, waits=(tk_in,))
        p.drain(A)
        tk_sig2 = act(sig_cd[:], scd_[:], AF.Sigmoid, waits=(tk_in,))

        stg_tok = {0: None, 1: None}
        stg_i = [0]

        def stage_store(src_slot_tv_fn, dst, bf):
            b = stg_i[0] % 2
            stg_i[0] += 1
            stg = (wqstg[:, b * 512:b * 512 + 128] if bf
                   else s512[4 + b][:, 0:128])
            tv = src_slot_tv_fn(stg, stg_tok[b])
            td = p.dma(G, st_p[b], dst, stg, waits=(tv,))
            stg_tok[b] = td
            return td

        pw_tok = [None, None]
        for wi in range(4):
            for oc in range(FC):
                b = oc % 2
                ld = p.dma(S, ld_p[b], pw[b][:, 0:D],
                           w_attn_d[wi][oc * 128:oc * 128 + 128, :],
                           waits=(pw_tok[b],))
                for fc in range(FC):
                    slot = fc % 4
                    sl = tslot(slot)
                    tk = tr_(sl, pw[b][:, fc * 128:fc * 128 + 128],
                             waits=(ld, pt_last[slot], pt_last[(slot + 2) % 4],
                                    tk_i))

                    def mk(sl_=sl, tk_=tk):
                        def f(stg, st_tok):
                            return vcopy(stg, sl_, waits=(tk_, st_tok))
                        return f
                    tv_holder = {}

                    def fn(stg, st_tok, sl_=sl, tk_=tk, hold=tv_holder):
                        tv = vcopy(stg, sl_, waits=(tk_, st_tok))
                        hold["tv"] = tv
                        return tv
                    stage_store(fn, wt_scr[wi][fc, oc], bf=False)
                    pt_last[fc % 4] = tv_holder["tv"]
                pw_tok[b] = (ts, p.cnt[ts])

        def prep_quant_weight(w_dram, n_oc, in_dim, dst_fn, scale_sb):
            stage_q = [None]
            for oc in range(n_oc):
                b = oc % 2
                ld = p.dma(S, ld_p[b], pw[b][:, 0:in_dim],
                           w_dram[oc * 128:oc * 128 + 128, :],
                           waits=(pw_tok[b],))
                rm = vred(ssm[5], pw[b][:, 0:in_dim], OP.max, absv=True,
                          waits=(ld,))
                p.drain(V)
                vts(scale_sb[:, oc:oc + 1], ssm[5][:, 0:1], 1.0 / 127.0, None,
                    OP.mult)
                vrecip(ssm[6], ssm[5])
                p.drain(V)
                t3 = vts(ssm[7], ssm[6][:, 0:1], 127.0, None, OP.mult)
                for seg in range(0, in_dim, 512):
                    wdt = min(512, in_dim - seg)
                    tq = quant_chain(s1k[3][:, 0:wdt], pw[b][:, seg:seg + wdt],
                                     ssm[7][:, 0:1], s1k[4][:, 0:wdt],
                                     s1k[5][:, 0:wdt], s1k[2][:, 0:wdt],
                                     i32t[:, 0:wdt], waits=(t3, stage_q[0]))
                    for k in range(wdt // 128):
                        fc = (seg + k * 128) // 128
                        slot = fc % 4
                        sl = tslot(slot)
                        tk = tr_(sl, s1k[3][:, k * 128:k * 128 + 128],
                                 waits=(tq, pt_last[slot],
                                        pt_last[(slot + 2) % 4], tk_i))
                        hold = {}

                        def fn(stg, st_tok, sl_=sl, tk_=tk, h=hold):
                            tv = vcopy(stg, sl_, waits=(tk_, st_tok))
                            h["tv"] = tv
                            return tv
                        stage_store(fn, dst_fn(fc, oc), bf=True)
                        pt_last[slot] = hold["tv"]
                    stage_q[0] = (ts, p.cnt[ts])
                pw_tok[b] = (vs, p.cnt[vs])

        prep_quant_weight(wd_d, FC, DFF, lambda fc, oc: wdq_scr[fc, oc], sd_sb)

        prep_done2 = ((st_p[0], p.cnt[st_p[0]]), (st_p[1], p.cnt[st_p[1]]))
        p.barrier(bar)

        fpw_tok = [None]

        def fprep_gen():
            for w_dram, dst_fn, scale_sb in (
                    (wg_d, lambda fc, oc: wgq_scr[fc, oc], sg_sb),
                    (wu_d, lambda fc, oc: wuq_scr[fc, oc], su_sb)):
                stage_q = [None]
                for oc in range(IC):
                    ld = p.dma(S, ld_p[0], fpw[:, 0:D],
                               w_dram[oc * 128:oc * 128 + 128, :],
                               waits=(fpw_tok[0],))
                    vred(ssm[5], fpw[:, 0:D], OP.max, absv=True, waits=(ld,))
                    p.drain(V)
                    vts(scale_sb[:, oc:oc + 1], ssm[5][:, 0:1], 1.0 / 127.0,
                        None, OP.mult)
                    vrecip(ssm[6], ssm[5])
                    p.drain(V)
                    t3 = vts(ssm[7], ssm[6][:, 0:1], 127.0, None, OP.mult)
                    for seg in range(0, D, 512):
                        wdt = min(512, D - seg)
                        tq = quant_chain(s512[3][:, 0:wdt],
                                         fpw[:, seg:seg + wdt],
                                         ssm[7][:, 0:1], s512[0][:, 0:wdt],
                                         s512[1][:, 0:wdt], s512[2][:, 0:wdt],
                                         i32t[:, 0:wdt],
                                         waits=(t3, stage_q[0]))
                        for k in range(wdt // 128):
                            fc = (seg + k * 128) // 128
                            slot = fc % 4
                            sl = tslot(slot)
                            tk = tr_(sl, s512[3][:, k * 128:k * 128 + 128],
                                     waits=(tq, pt_last[slot],
                                            pt_last[(slot + 2) % 4], tk_i))
                            hold = {}

                            def fn(stg, st_tok_, sl_=sl, tk_=tk, h=hold):
                                tv = vcopy(stg, sl_, waits=(tk_, st_tok_))
                                h["tv"] = tv
                                return tv
                            stage_store(fn, dst_fn(fc, oc), bf=True)
                            pt_last[slot] = hold["tv"]
                        stage_q[0] = (ts, p.cnt[ts])
                    fpw_tok[0] = (vs, p.cnt[vs])
                    yield

        fgen = fprep_gen()

        for s_i in range(NSEQ):
            xt_bufs = [s1k[0], s1k[1]]
            xt_tok = [None, None]
            for tt in range(TT):
                b = tt % 2
                ld = p.dma(S, ld_x[b], xt_bufs[b][:, 0:D],
                           x_d[s_i, tt * 128:tt * 128 + 128, :],
                           waits=(xt_tok[b],))
                for fc in range(FC):
                    transpose_to(fc % 4, xT[:, fc, tt * 128:tt * 128 + 128],
                                 xt_bufs[b][:, fc * 128:fc * 128 + 128],
                                 waits=(ld,))
                xt_tok[b] = (ts, p.cnt[ts])
                next(fgen, None)
            xT_ready = (vs, p.cnt[vs])

            ws_tok = [None, None]
            head_done = [None]
            for hp in range(NHP):
                qk_evacs = []
                for pi, name in ((0, "q"), (1, "k"), (2, "v")):
                    b = (hp * 3 + pi) % 2
                    wld = p.dma(S, ld_w[b], ws[:, b, 0:FC, :],
                                wt_scr[pi][:, hp].rearrange("a p c -> p a c"),
                                waits=(ws_tok[b],) + prep_done2)
                    if name in ("q", "k"):
                        dst = qt if name == "q" else kt
                        for st_ in range(SL // OSW):
                            pb = st_ % 2
                            tk = None
                            for fc in range(FC):
                                tk = mm(p_big[pb][:, 0:OSW], ws[:, b, fc, :],
                                        xT[:, fc, st_ * OSW:st_ * OSW + OSW],
                                        start=(fc == 0), stop=(fc == FC - 1),
                                        waits=(wld, xT_ready, pb_last[pb])
                                        if fc == 0 else ())
                            vcopy(dst[0:64, st_ * OSW:st_ * OSW + OSW],
                                  p_big[pb][0:64, 0:OSW], waits=(tk,))
                            tv1 = vcopy(
                                dst[64:128, SL + st_ * OSW:SL + st_ * OSW + OSW],
                                p_big[pb][64:128, 0:OSW], waits=())
                            pb_last[pb] = tv1
                            qk_evacs.append(tv1)
                        ws_tok[b] = (ts, p.cnt[ts])
                    else:
                        for gt in range(TT // 4):
                            pb = gt % 2
                            tk = None
                            for t4 in range(4):
                                tt = gt * 4 + t4
                                for fc in range(FC):
                                    tk = mm(p_big[pb][:, t4 * 128:t4 * 128 + 128],
                                            xT[:, fc, tt * 128:tt * 128 + 128],
                                            ws[:, b, fc, :],
                                            start=(t4 == 0 and fc == 0),
                                            stop=(t4 == 3 and fc == FC - 1),
                                            waits=(wld, xT_ready, pb_last[pb])
                                            if (t4 == 0 and fc == 0) else (),
                                            skip=True)
                            tvl = None
                            for t4 in range(4):
                                tt = gt * 4 + t4
                                vcopy(v65[:, tt, 0:64],
                                      p_big[pb][:, t4 * 128:t4 * 128 + 64],
                                      waits=(tk,))
                                tvl = vcopy(v65[:, tt, 65:129],
                                            p_big[pb][:, t4 * 128 + 64:
                                                       t4 * 128 + 128])
                            pb_last[pb] = tvl
                            qk_evacs.append(tvl)
                        ws_tok[b] = (ts, p.cnt[ts])
                qk_ready = (vs, p.cnt[vs])

                for h in range(2):
                    g = 2 * hp + h
                    col0 = h * SL
                    prev_head = head_done[0]
                    pv_tok = {}
                    for kc in range(KC):
                        eb = kc % 2
                        q0 = kc * 128
                        exps = []
                        while q0 < SL:
                            wdt = min(512, SL - q0)
                            pb = len(exps) % 2
                            w0 = (qk_ready, pb_last[pb], pv_tok.get(kc - 2))
                            tkm = mm(p_big[pb][:, 0:wdt],
                                     kt[:, col0 + kc * 128:col0 + kc * 128 + 128],
                                     qt[:, col0 + q0:col0 + q0 + wdt],
                                     start=True, stop=True, waits=w0)
                            if q0 == kc * 128:
                                tkm = vtt(p_big[pb][:, 0:128], p_big[pb][:, 0:128],
                                          mask[:], OP.add, waits=(tkm, tk_m))
                            tke = act(e_sb[eb][:, q0 - kc * 128:
                                               q0 - kc * 128 + wdt],
                                      p_big[pb][:, 0:wdt], AF.Exp,
                                      bias=consts[:, CB_EXP:CB_EXP + 1],
                                      scale=0.125, waits=(tkm,))
                            pb_last[pb] = tke
                            exps.append(tke)
                            q0 += wdt
                        tkpv = None
                        for qi in range(kc, KC):
                            w0 = tuple(exps) if qi == kc else ()
                            if qi == kc and kc == 0 and prev_head is not None:
                                w0 = w0 + (prev_head,)
                            tkpv = mm(p_pv[:, qi * 128:qi * 128 + 65],
                                      e_sb[eb][:, (qi - kc) * 128:
                                               (qi - kc) * 128 + 128],
                                      v65[:, kc, h * 65:h * 65 + 65],
                                      start=(kc == 0 and qi % 4 == 0),
                                      stop=(qi == kc), waits=w0, skip=True)
                        pv_tok[kc] = tkpv
                    for qi in range(KC):
                        ab = qi % 2
                        p.drain(V)
                        vrecip(ssm[0], p_pv[:, qi * 128 + 64:qi * 128 + 65],
                               waits=(pv_tok[KC - 1], stg_tok[ab]))
                        p.drain(V)
                        tv = vts(astg[ab][:], p_pv[:, qi * 128:qi * 128 + 64],
                                 ssm[0][:, 0:1], None, OP.mult)
                        td = p.dma(G, st_at[ab], attn_scr[g, qi], astg[ab][:],
                                   waits=(tv,))
                        stg_tok[ab] = td
                    head_done[0] = (vs, p.cnt[vs])
            attn_done2 = ((st_at[0], p.cnt[st_at[0]]),
                          (st_at[1], p.cnt[st_at[1]]))
            p.barrier(bar)

            wo_ld = None
            for fc in range(FC):
                wo_ld = p.dma(S, ld_w[0], wo_view[:, fc, 0:D],
                              wt_scr[3][fc].rearrange("c p d -> p c d"),
                              waits=prep_done2 + (ws_tok[0], ws_tok[1])
                              if fc == 0 else ())
            asm_bufs = [a2gap[0], a2gap[1]]
            att_bufs = [a2gap[2], a2gap[3]]
            xt2_bufs = [a2gap[4], a2gap[5]]
            r1pre, tcb, sqb, r1out = s1k[2], s1k[3], s1k[4], s1k[5]
            btok = {}
            prev_ln = None
            for tt in range(TT):
                b = tt % 2
                lda = p.dma(S, ld_at[b], asm_bufs[b][:, 0:NH * 64],
                            attn_scr[:, tt].rearrange("a p c -> p a c"),
                            waits=attn_done2 + (btok.get(("asm", b)),))
                ldx = p.dma(S, ld_x[b], xt2_bufs[b][:, 0:D],
                            x_d[s_i, tt * 128:tt * 128 + 128, :],
                            waits=(btok.get(("xt2", b)),))
                trv = None
                for fc in range(FC):
                    trv = transpose_to(fc % 4,
                                       att_bufs[b][:, fc * 128:fc * 128 + 128],
                                       asm_bufs[b][:, fc * 128:fc * 128 + 128],
                                       waits=(lda, btok.get(("att", b))))
                ot = []
                for os_ in range(D // OSW):
                    pb = os_ % 2
                    tk = None
                    for fc in range(FC):
                        tk = mm(p_big[pb][:, 0:OSW],
                                att_bufs[b][:, fc * 128:fc * 128 + 128],
                                wo_view[:, fc, os_ * OSW:os_ * OSW + OSW],
                                start=(fc == 0), stop=(fc == FC - 1),
                                waits=(trv, wo_ld, pb_last[pb])
                                if fc == 0 else ())
                    tv = vtt(r1pre[:, os_ * OSW:os_ * OSW + OSW],
                             p_big[pb][:, 0:OSW],
                             xt2_bufs[b][:, os_ * OSW:os_ * OSW + OSW],
                             OP.add, waits=(tk, ldx, btok.get(("r1p",))))
                    pb_last[pb] = tv
                    ot.append(tv)
                btok[("att", b)] = (ts, p.cnt[ts])
                tln, tA = ln_chain(r1pre[:, 0:D], r1out[:, 0:D], tcb, sqb,
                                   waits=tuple(ot) + (btok.get(("r1o",)),),
                                   prev=prev_ln)
                prev_ln = tln
                btok[("r1p",)] = tA
                tvm = vred(rmax_all[:, s_i * TT + tt:s_i * TT + tt + 1],
                           r1out[:, 0:D], OP.max, absv=True, waits=(tln,))
                td = p.dma(G, st_r1, r1_scr[s_i * TT + tt], r1out[:, 0:D],
                           waits=(tvm,))
                btok[("r1o",)] = td
                btok[("asm", b)] = (ts, p.cnt[ts])
                btok[("xt2", b)] = (vs, p.cnt[vs])
            p.barrier(bar)

        for _ in fgen:
            pass
        prep_done3 = ((st_p[0], p.cnt[st_p[0]]), (st_p[1], p.cnt[st_p[1]]))
        r1_done = (st_r1, p.cnt[st_r1])

        def global_max_and_scales(src_cols, ncols, sig_tok, sig_sb, cc_i, cc_o,
                                  g_sb, bc_a, bc_b, scls):
            p.drain(V)
            t0 = vred(ssm[0], src_cols[:, 0:ncols], OP.max)
            tt0 = tr_(p_misc[0:1, 0:128], ssm[0][:, 0:1],
                      waits=(t0, tk_i, pm_last[0]))
            t1 = vred(lmax[:], p_misc[0:1, 0:128], OP.max, waits=(tt0,))
            td1 = p.dma(G, ccd, cc_i, lmax[:], waits=(t1,))
            tcc = p.op(G, lambda e: e.collective_compute(
                "AllReduce", OP.max, replica_groups=[list(range(cfg.NC))],
                ins=[cc_i.opt()], outs=[cc_o.opt()]), waits=(td1,), inc=ccch)
            td2 = p.dma(G, ccd, g_sb[:], cc_o, waits=(tcc,))
            vtt(tmp11[0][:], sig_sb[:], g_sb[:], OP.mult, waits=(td2, sig_tok))
            p.drain(V)
            ts1 = vts(tmp11[1][:], tmp11[0][:], 1.0 / 127.0, None, OP.mult)
            vrecip(tmp11[2][:], tmp11[0][:])
            p.drain(V)
            ts2 = vts(tmp11[3][:], tmp11[2][:], 127.0, None, OP.mult)
            tb1 = mm(p_misc[:, 128:129], ones_row[:], tmp11[1][:], True, True,
                     waits=(ts1, tk_o, t1))
            tv1 = vcopy(bc_a[:], p_misc[:, 128:129], waits=(tb1,))
            tb2 = mm(p_misc[:, 256:257], ones_row[:], tmp11[3][:], True, True,
                     waits=(ts2, tv1))
            tv2 = vcopy(bc_b[:], p_misc[:, 256:257], waits=(tb2,))
            pm_last[0] = tv2
            p.drain(V)
            for dst, src in scls:
                vts(dst, src, bc_a[:, 0:1], None, OP.mult, waits=(tv1, tv2))
            p.drain(V)

        global_max_and_scales(rmax_all, NT16, tk_sig1, sig_cx, cc_io[0],
                              cc_io[1], gmax[0], sx_bc, rsx_bc,
                              [(gsc_sb[:, 0:IC], sg_sb[:, 0:IC]),
                               (usc_sb[:, 0:IC], su_sb[:, 0:IC])])
        p.barrier(bar)

        hcol = [0]
        r1b = [s1k[0], s1k[1]]
        rb_tok = [None, None]
        wsl_tok = [None, None]
        st_tok = {}
        xq_last_mm = [None] * NXQ
        xq_done = [None, None]
        xq_ready_buf = [None] * NXQ

        def f1_stage(blk):
            b2 = blk % 2
            bq = blk % NXQ
            xcol = b2 * FBLK
            ntt = FBLK // 128
            for k in range(ntt):
                b = (blk * ntt + k) % 2
                ld = p.dma(S, ld_r1[b], r1b[b][:, 0:D], r1_scr[blk * ntt + k],
                           waits=(r1_done, rb_tok[b]))
                for fc in range(FC):
                    transpose_to(fc % 4,
                                 xT[:, fc, xcol + k * 128:xcol + k * 128 + 128],
                                 r1b[b][:, fc * 128:fc * 128 + 128],
                                 waits=(ld, xq_done[b2]))
                rb_tok[b] = (ts, p.cnt[ts])
            xTf_ready = (vs, p.cnt[vs])
            tq = None
            for fc in range(FC):
                tq = quant_chain(xqT[bq][:, fc, 0:FBLK],
                                 xT[:, fc, xcol:xcol + FBLK],
                                 rsx_bc[:, 0:1], s1k[2][:, 0:FBLK],
                                 s1k[3][:, 0:FBLK], s1k[4][:, 0:FBLK],
                                 i32t[:, 0:FBLK],
                                 waits=(xTf_ready, xq_last_mm[bq])
                                 + prep_done3)
            xq_done[b2] = tq
            xq_ready_buf[bq] = tq

        ppv_last = {0: None, 1: None}
        for blk in range(min(2, NBLK)):
            f1_stage(blk)
        bp = 0
        while 2 * bp < NBLK:
            blocks = [2 * bp] + ([2 * bp + 1] if 2 * bp + 1 < NBLK else [])
            if 2 * bp + 2 < NBLK and NXQ > 2:
                f1_stage(2 * bp + 2)
            for ic in range(IC):
                b = ic % 2
                wld = p.dma(S, ld_w[b], wslg[:, b, 0:FC, :],
                            wgq_scr[:, ic].rearrange("a p c -> p a c"),
                            waits=(wsl_tok[b],) + prep_done3)
                wld = p.dma(S, ld_w[b], wslg[:, b, FC:2 * FC, :],
                            wuq_scr[:, ic].rearrange("a p c -> p a c"))
                for j, blk in enumerate(blocks):
                    bq = blk % NXQ
                    pg = p_big[0][:, 0:FBLK] if j == 0 else p_pv[:, 0:FBLK]
                    pu = (p_big[1][:, 0:FBLK] if j == 0
                          else p_pv[:, FBLK:2 * FBLK])
                    gl = pb_last[0] if j == 0 else ppv_last[0]
                    ul = pb_last[1] if j == 0 else ppv_last[1]
                    tkg = tku = None
                    for fc in range(FC):
                        tkg = mm(pg, wslg[:, b, fc, :],
                                 xqT[bq][:, fc, 0:FBLK],
                                 start=(fc == 0), stop=(fc == FC - 1),
                                 waits=(wld, xq_ready_buf[bq], gl)
                                 if fc == 0 else ())
                    for fc in range(FC):
                        tku = mm(pu, wslg[:, b, FC + fc, :],
                                 xqT[bq][:, fc, 0:FBLK],
                                 start=(fc == 0), stop=(fc == FC - 1),
                                 waits=(ul,) if fc == 0 else ())
                    tas = act(s512[j][:, 0:FBLK], pg, AF.Silu,
                              scale=gsc_sb[:, ic:ic + 1],
                              waits=(tkg, st_tok.get(("h", j))))
                    tvu = vts(s512[2 + j][:, 0:FBLK], pu,
                              usc_sb[:, ic:ic + 1], None, OP.mult,
                              waits=(tku, st_tok.get(("h", j))))
                    if j == 0:
                        pb_last[0] = tas
                        pb_last[1] = tvu
                    else:
                        ppv_last[0] = tas
                        ppv_last[1] = tvu
                    hst = s512[4 + j]
                    tvh = vtt(hst[:, 0:FBLK], s512[j][:, 0:FBLK],
                              s512[2 + j][:, 0:FBLK],
                              OP.mult, waits=(tas, tvu,
                                              st_tok.get(("hst", j))))
                    st_tok[("h", j)] = tvh
                    tvm = vred(hmax_all[:, hcol[0]:hcol[0] + 1],
                               hst[:, 0:FBLK],
                               OP.max, absv=True, waits=(tvh,))
                    hcol[0] += 1
                    tdh = p.dma(G, st_h[j],
                                h_scr[ic][:, blk * FBLK:blk * FBLK + FBLK],
                                hst[:, 0:FBLK], waits=(tvm,))
                    st_tok[("hst", j)] = tdh
                wsl_tok[b] = (ts, p.cnt[ts])
            for blk in blocks:
                xq_last_mm[blk % NXQ] = (ts, p.cnt[ts])
            if 2 * bp + 3 < NBLK and NXQ > 2:
                f1_stage(2 * bp + 3)
            elif 2 * bp + 2 < NBLK and NXQ == 2:
                f1_stage(2 * bp + 2)
                if 2 * bp + 3 < NBLK:
                    f1_stage(2 * bp + 3)
            bp += 1
        p.barrier(bar)
        h_done2 = ((st_h[0], p.cnt[st_h[0]]), (st_h[1], p.cnt[st_h[1]]))

        global_max_and_scales(hmax_all, hcol[0], tk_sig2, sig_cd, cc_io[2],
                              cc_io[3], gmax[1], s2_bc, rq2_bc,
                              [(dsc_sb[:, 0:FC], sd_sb[:, 0:FC])])
        p.barrier(bar)

        NTK = FSP // 128
        hlb = [s1k[0], s1k[1]]
        hb_tok = [None, None]
        wdl_tok = [None, None]
        dstg = [s512[4], s512[5]]
        dstg_tok = [None, None]
        hq_last_mm = [None, None]
        rb_tok = {}
        prev_ln2 = None
        r1b = [f2v[5], f2v[6]]
        yb = [f2v[0], f2v[1]]
        r2pre, tcb, sqb = f2v[2], f2v[3], f2v[4]
        hq_ready_buf = [None] * NHQ
        hq_last_mm = [None] * NHQ
        f2_pair = [0]

        def f2_stage(sp):
            bq = sp % NHQ
            tq = None
            ic = 0
            while ic < IC:
                npair = min(max(512 // FSP, 1), IC - ic)
                wdt = FSP * npair
                b = f2_pair[0] % 2
                f2_pair[0] += 1
                ld = None
                for j in range(npair):
                    ld = p.dma(S, ld_h[b], hlb[b][:, j * FSP:j * FSP + FSP],
                               h_scr[ic + j][:, sp * FSP:sp * FSP + FSP],
                               waits=h_done2 + (hb_tok[b],) if j == 0 else ())
                tq = quant_chain(
                    hqT[bq][:].rearrange("p a b -> p (a b)")[:, ic * FSP:
                                                             ic * FSP + wdt],
                    hlb[b][:, 0:wdt], rq2_bc[:, 0:1],
                    s1k[2][:, 0:wdt], s1k[3][:, 0:wdt], s1k[4][:, 0:wdt],
                    i32t[:, 0:wdt], waits=(ld, hq_last_mm[bq]))
                hb_tok[b] = tq
                ic += npair
            hq_ready_buf[bq] = tq

        for sp in range(min(2, NSP)):
            f2_stage(sp)
        bp = 0
        while 2 * bp < NSP:
            strips = [2 * bp] + ([2 * bp + 1] if 2 * bp + 1 < NSP else [])
            if 2 * bp + 2 < NSP and NHQ > 2:
                f2_stage(2 * bp + 2)
            for oc in range(FC):
                b = oc % 2
                wl = p.dma(S, ld_w[b], wsld[:, b, 0:IC, :],
                           wdq_scr[:, oc].rearrange("a p c -> p a c"),
                           waits=(wdl_tok[b],) + prep_done3)
                for j, sp in enumerate(strips):
                    bq = sp % NHQ
                    tk = None
                    for ic in range(IC):
                        tk = mm(p_big[j][:, 0:FSP], wsld[:, b, ic, :],
                                hqT[bq][:, ic, 0:FSP], start=(ic == 0),
                                stop=(ic == IC - 1),
                                waits=(wl, hq_ready_buf[bq], pb_last[j])
                                if ic == 0 else ())
                    tvd = vts(dstg[j][:, 0:FSP], p_big[j][:, 0:FSP],
                              dsc_sb[:, oc:oc + 1], None, OP.mult,
                              waits=(tk, dstg_tok[j]))
                    pb_last[j] = tvd
                    for k in range(NTK):
                        kk = j * NTK + k
                        transpose_to((oc * 2 * NTK + kk) % 4,
                                     outpre[:, kk, oc * 128:oc * 128 + 128],
                                     dstg[j][:, k * 128:k * 128 + 128],
                                     waits=(tvd, rb_tok.get(("opre", kk))))
                    dstg_tok[j] = (ts, p.cnt[ts])
                wdl_tok[b] = (ts, p.cnt[ts])
            for sp in strips:
                hq_last_mm[sp % NHQ] = (ts, p.cnt[ts])
            if 2 * bp + 3 < NSP and NHQ > 2:
                f2_stage(2 * bp + 3)
            elif 2 * bp + 2 < NSP and NHQ == 2:
                f2_stage(2 * bp + 2)
                if 2 * bp + 3 < NSP:
                    f2_stage(2 * bp + 3)
            outpre_ready = (vs, p.cnt[vs])
            for k in range(len(strips) * NTK):
                b = k % 2
                gi = 2 * bp * NTK + k
                ld = p.dma(S, ld_r1[b], r1b[b][:, 0:D], r1_scr[gi],
                           waits=(r1_done, rb_tok.get(("rb", b))))
                tv = vtt(r2pre[:, 0:D], outpre[:, k, 0:D], r1b[b][:, 0:D],
                         OP.add, waits=(ld, outpre_ready, rb_tok.get(("r2",))))
                rb_tok[("opre", k)] = tv
                tln, tA = ln_chain(r2pre[:, 0:D], yb[b][:, 0:D], tcb, sqb,
                                   waits=(tv, rb_tok.get(("yb", b))),
                                   prev=prev_ln2)
                prev_ln2 = tln
                rb_tok[("r2",)] = tA
                s_idx = gi // TT
                t_idx = gi % TT
                td = p.dma(G, st_y[b], y_d[s_idx, t_idx * 128:t_idx * 128 + 128, :],
                           yb[b][:, 0:D], waits=(tln,))
                rb_tok[("yb", b)] = td
                rb_tok[("rb", b)] = tln
            bp += 1
        p.barrier(bar)

        p.op(G, lambda e: e.nop(),
             waits=((st_y[0], p.cnt[st_y[0]]), (st_y[1], p.cnt[st_y[1]])))

        _safe = SAFE_EVERY
        if _safe > 0:
            newops = []
            cnt2 = 0
            bcnt = {e: 0 for e in ENGINES}
            for rec in p.ops:
                newops.append(rec)
                if rec[3] is not None and rec[3][0].startswith("bar_"):
                    bcnt[rec[3][0][4:]] += 1
                cnt2 += 1
                if cnt2 % _safe == 0:
                    vals = {}
                    for e in ENGINES:
                        bcnt[e] += 1
                        vals[e] = bcnt[e]
                        newops.append((e, lambda eng: eng.nop(), (), (bar[e], 1)))
                    for e in ENGINES:
                        w = tuple((bar[o], vals[o]) for o in ENGINES if o != e)
                        newops.append((e, lambda eng: eng.nop(), w, None))
            p.ops = newops
            for e in ENGINES:
                p.cnt[bar[e]] = bcnt[e]
        p.compress((ts, vs, as_))
        p.check()
        _LAST_PLAN[:] = [p]

        with nc.Block() as block:
            def make(eng_name):
                def body(eng):
                    waited = {}
                    for rec in p.ops:
                        if rec[0] != eng_name:
                            continue
                        for s_, v_ in rec[2]:
                            if waited.get(s_, 0) >= v_:
                                continue
                            eng.wait_ge(sems[s_], v_)
                            waited[s_] = v_
                        inst = rec[1](eng)
                        if rec[3] is not None:
                            inst.then_inc(sems[rec[3][0]], rec[3][1])
                return body
            for e_name in ENGINES:
                getattr(block, e_name)(make(e_name))

    return nc


def host_inputs(cfg, x_shard, Wq, Wk, Wv, Wo, clip_x, clip_down,
                w_gate, w_up, w_down):
    consts = np.zeros((128, 8), np.float32)
    consts[:, 0] = -4.0
    consts[:, 1] = 1e-5
    consts[:, 2] = 0.5
    consts[:, 3] = 1.0
    mask = np.where(np.arange(128)[:, None] <= np.arange(128)[None, :],
                    np.float32(0.0), np.float32(-1e9))
    return {
        "x": np.ascontiguousarray(x_shard, dtype=np.float32),
        "wq": np.ascontiguousarray(Wq, dtype=np.float32),
        "wk": np.ascontiguousarray(Wk, dtype=np.float32),
        "wv": np.ascontiguousarray(Wv, dtype=np.float32),
        "wo": np.ascontiguousarray(Wo, dtype=np.float32),
        "wg": np.ascontiguousarray(w_gate, dtype=np.float32),
        "wu": np.ascontiguousarray(w_up, dtype=np.float32),
        "wd": np.ascontiguousarray(w_down, dtype=np.float32),
        "clipx": np.asarray(clip_x, np.float32).reshape(1, 1),
        "clipd": np.asarray(clip_down, np.float32).reshape(1, 1),
        "ident": np.eye(128, dtype=np.float32),
        "mask": np.ascontiguousarray(mask),
        "consts": consts,
        "ones_row": np.ones((1, 128), np.float32),
    }


def kernel(L0_attn_x, L0_attn_Wq, L0_attn_Wk, L0_attn_Wv, L0_attn_Wo,
           L0_ffn_clip_x, L0_ffn_clip_down, L0_ffn_w_gate, L0_ffn_w_up,
           L0_ffn_w_down):
    cfg = Cfg()
    x = np.asarray(L0_attn_x, np.float32)
    per = x.shape[0] // cfg.NC
    nc = build_nc(cfg)
    in_maps = [host_inputs(cfg, x[c * per:(c + 1) * per], L0_attn_Wq,
                           L0_attn_Wk, L0_attn_Wv, L0_attn_Wo, L0_ffn_clip_x,
                           L0_ffn_clip_down, L0_ffn_w_gate, L0_ffn_w_up,
                           L0_ffn_w_down)
               for c in range(cfg.NC)]
    res = run_bass_kernel_spmd(nc, in_maps, core_ids=list(range(cfg.NC)))
    out = np.concatenate(
        [np.asarray(r["y"], np.float32).reshape(per, cfg.S, cfg.D)
         for r in res.results], axis=0)
    return out


# version 15 (final)
# speedup vs baseline: 1.0630x; 1.0539x over previous
import contextlib
import numpy as np

import concourse.bass as bass
import concourse.mybir as mybir
from concourse.bass_utils import run_bass_kernel_spmd

F32 = mybir.dt.float32
BF16 = mybir.dt.float16
AF = mybir.ActivationFunctionType
OP = mybir.AluOpType
AXX = mybir.AxisListType.X

ENGINES = ("sync", "tensor", "vector", "scalar", "gpsimd")
SAFE_EVERY = 0
_LAST_PLAN = []
T, V, A, S, G = "tensor", "vector", "scalar", "sync", "gpsimd"


class Cfg:
    def __init__(self, nseq=4, s=2048, d=1024, dff=2816, ncores=8):
        self.NSEQ = nseq
        self.S = s
        self.D = d
        self.DFF = dff
        self.NC = ncores
        self.FC = d // 128
        self.IC = dff // 128
        self.TT = s // 128
        self.KC = s // 128
        self.NH = d // 64
        self.TOK = nseq * s
        self.NT16 = self.TOK // 128
        self.FBLK = min(512, self.TOK)
        self.NBLK = self.TOK // self.FBLK
        self.FSP = 256 if self.TOK >= 512 else self.TOK
        self.NSP = self.TOK // self.FSP


class Plan:

    def __init__(self):
        self.ops = []
        self.cnt = {}
        self.sem_names = []
        self.eng_sem = {}
        self.pending = {}

    def sem(self, name):
        self.sem_names.append(name)
        self.cnt[name] = 0
        return name

    def op(self, eng, fn, waits=(), inc=None, amt=1):
        waits = tuple(w for w in waits if w is not None)
        pend = self.pending.pop(eng, None)
        if pend is not None:
            waits = waits + (pend,)
        tok = None
        if inc is not None:
            self.cnt[inc] += amt
            tok = (inc, self.cnt[inc])
        self.ops.append((eng, fn, waits, (inc, amt) if inc else None))
        return tok

    def dma(self, eng, ch, out, in_, waits=()):
        return self.op(eng, lambda e, o=out, i=in_: e.dma_start(out=o, in_=i),
                       waits=waits, inc=ch, amt=16)

    def drain(self, eng):
        sem = self.eng_sem.get(eng)
        if sem is None or self.cnt[sem] == 0:
            self.op(eng, lambda e: e.drain())
            return
        self.pending[eng] = (sem, self.cnt[sem])

    def barrier(self, bar):
        vals = {}
        for e in ENGINES:
            self.cnt[bar[e]] += 1
            vals[e] = self.cnt[bar[e]]
            self.ops.append((e, lambda eng: eng.nop(), (), (bar[e], 1)))
        for e in ENGINES:
            waits = tuple((bar[o], vals[o]) for o in ENGINES if o != e)
            self.ops.append((e, lambda eng: eng.nop(), waits, None))

    def compress(self, sems_to_strip):
        waited = {}
        for _, _, waits, _ in self.ops:
            for s_, v_ in waits:
                waited.setdefault(s_, set()).add(v_)
        cnt = {s_: 0 for s_ in self.cnt}
        kept = {s_: 0 for s_ in self.cnt}
        remap = {}
        newops = []
        for eng, fn, waits, inc in self.ops:
            w2 = tuple((s_, remap.get((s_, v_), v_)) for s_, v_ in waits)
            inc2 = inc
            if inc is not None:
                s_, amt = inc
                cnt[s_] += amt
                if s_ in sems_to_strip:
                    if cnt[s_] in waited.get(s_, ()):
                        kept[s_] += amt
                        inc2 = (s_, amt)
                    else:
                        inc2 = None
                    remap[(s_, cnt[s_])] = kept[s_]
                else:
                    kept[s_] += amt
            newops.append((eng, fn, w2, inc2))
        self.ops = newops

    def check(self):
        q = {e: [i for i, rec in enumerate(self.ops) if rec[0] == e] for e in ENGINES}
        pos = {e: 0 for e in ENGINES}
        sem = {s_: 0 for s_ in self.cnt}
        progress = True
        while progress:
            progress = False
            for e in ENGINES:
                while pos[e] < len(q[e]):
                    rec = self.ops[q[e][pos[e]]]
                    if any(sem[s_] < v for s_, v in rec[2]):
                        break
                    if rec[3]:
                        sem[rec[3][0]] += rec[3][1]
                    pos[e] += 1
                    progress = True
        stuck = {e: len(q[e]) - pos[e] for e in ENGINES if pos[e] < len(q[e])}
        if stuck:
            details = []
            for e in stuck:
                rec = self.ops[q[e][pos[e]]]
                missing = [(s_, v, sem[s_]) for s_, v in rec[2] if sem[s_] < v]
                details.append(f"{e}@{pos[e]}/{len(q[e])} waiting {missing}")
            raise RuntimeError("plan deadlock: " + "; ".join(details))


def build_nc(cfg: Cfg):
    nc = bass.Bass("TRN2", target_bir_lowering=False)
    NSEQ, SL, D, DFF = cfg.NSEQ, cfg.S, cfg.D, cfg.DFF
    FC, IC, TT, KC, NH = cfg.FC, cfg.IC, cfg.TT, cfg.KC, cfg.NH
    TOK, NT16 = cfg.TOK, cfg.NT16
    FBLK, NBLK, NSP, FSP = cfg.FBLK, cfg.NBLK, cfg.NSP, cfg.FSP
    NHP = NH // 2
    OSW = min(512, D)

    x_d = nc.dram_tensor("x", [NSEQ, SL, D], F32, kind="ExternalInput").ap()
    w_attn_d = [nc.dram_tensor(n, [D, D], F32, kind="ExternalInput").ap()
                for n in ("wq", "wk", "wv", "wo")]
    wg_d = nc.dram_tensor("wg", [DFF, D], F32, kind="ExternalInput").ap()
    wu_d = nc.dram_tensor("wu", [DFF, D], F32, kind="ExternalInput").ap()
    wd_d = nc.dram_tensor("wd", [D, DFF], F32, kind="ExternalInput").ap()
    clipx_d = nc.dram_tensor("clipx", [1, 1], F32, kind="ExternalInput").ap()
    clipd_d = nc.dram_tensor("clipd", [1, 1], F32, kind="ExternalInput").ap()
    ident_d = nc.dram_tensor("ident", [128, 128], F32, kind="ExternalInput").ap()
    mask_d = nc.dram_tensor("mask", [128, 128], F32, kind="ExternalInput").ap()
    consts_d = nc.dram_tensor("consts", [128, 8], F32, kind="ExternalInput").ap()
    ones_d = nc.dram_tensor("ones_row", [1, 128], F32, kind="ExternalInput").ap()
    y_d = nc.dram_tensor("y", [NSEQ, SL, D], F32, kind="ExternalOutput").ap()

    wt_scr = [nc.dram_tensor(f"wt{i}", [FC, FC, 128, 128], F32).ap()
              for i in range(4)]
    wgq_scr = nc.dram_tensor("wgq", [FC, IC, 128, 128], BF16).ap()
    wuq_scr = nc.dram_tensor("wuq", [FC, IC, 128, 128], BF16).ap()
    wdq_scr = nc.dram_tensor("wdq", [IC, FC, 128, 128], BF16).ap()
    attn_scr = nc.dram_tensor("attns", [NH, TT, 128, 64], F32).ap()
    r1_scr = nc.dram_tensor("r1s", [NT16, 128, D], F32).ap()
    h_scr = nc.dram_tensor("hs", [IC, 128, TOK], F32).ap()
    cc_io = [nc.dram_tensor(f"cc{i}", [1, 1], F32).ap() for i in range(4)]

    CB_EXP, CB_EPS, CB_HALF, CB_ONE, CB_ZERO = 0, 1, 2, 3, 5

    p = Plan()
    ld_c = p.sem("ld_c")
    ld_p = [p.sem(f"ld_p{i}") for i in range(2)]
    st_p = [p.sem(f"st_p{i}") for i in range(2)]
    ld_w = [p.sem(f"ld_w{i}") for i in range(2)]
    ld_x = [p.sem(f"ld_x{i}") for i in range(2)]
    ld_r1 = [p.sem(f"ld_r1{i}") for i in range(2)]
    ld_h = [p.sem(f"ld_h{i}") for i in range(2)]
    ld_at = [p.sem(f"ld_at{i}") for i in range(2)]
    st_at = [p.sem(f"st_at{i}") for i in range(2)]
    st_r1 = p.sem("st_r1")
    st_h = [p.sem(f"st_h{i}") for i in range(2)]
    st_y = [p.sem(f"st_y{i}") for i in range(2)]
    ccch = p.sem("ccch"); ccd = p.sem("ccd")
    ts = p.sem("ts"); vs = p.sem("vs"); as_ = p.sem("as_")
    p.eng_sem = {T: ts, V: vs, A: as_}
    bar = {e: p.sem("bar_" + e) for e in ENGINES}

    with contextlib.ExitStack() as stk:
        ent = stk.enter_context

        def sbt(name, shape, dt=F32):
            return ent(nc.sbuf_tensor(name, shape, dt))

        AW = max(FC * SL, 2 * DFF, 4 * D)
        arena = sbt("arena", [128, AW])
        xT = arena[:, 0:FC * SL].rearrange("p (a b) -> p a b", a=FC)
        qt = sbt("qt", [128, 4 * SL], BF16)
        kt = sbt("kt", [128, 4 * SL], BF16)
        v65 = sbt("v65", [128, KC, 130])
        e_sb = [sbt(f"e{i}", [128, SL]) for i in range(2)]
        ws = sbt("ws", [128, 2, FC, 128])
        use_gaps = (FC >= 6 and SL >= 2 * D and AW >= 11 * D)
        n1k = 6 if use_gaps else 12
        s1k = [sbt(f"s1k{i}", [128, max(1024, D)]) for i in range(n1k)]
        if use_gaps:
            a2gap = [arena[:, i * SL + D:(i + 1) * SL] for i in range(6)]
            f2v = [arena[:, (4 + i) * D:(5 + i) * D] for i in range(7)]
        else:
            a2gap = [s1k[6], s1k[7], s1k[8], s1k[9], s1k[0], s1k[1]]
            f2v = [s1k[5], s1k[6], s1k[7], s1k[8], s1k[9], s1k[10], s1k[11]]
        s512 = [sbt(f"s512{i}", [128, 512]) for i in range(6)]
        ssm = [sbt(f"ssm{i}", [128, 1]) for i in range(8)]
        rmax_all = sbt("rmax_all", [128, max(NT16, 2)])
        nhc = max(IC * (FBLK // 512) * NBLK, 2)
        hmax_all = sbt("hmax_all", [128, nhc])
        sg_sb = sbt("sg", [128, IC]); su_sb = sbt("su", [128, IC])
        sd_sb = sbt("sd", [128, FC])
        gsc_sb = sbt("gsc", [128, IC]); usc_sb = sbt("usc", [128, IC])
        dsc_sb = sbt("dsc", [128, FC])
        consts = sbt("consts_sb", [128, 8])
        ident = sbt("ident_sb", [128, 128])
        ident_bf = sbt("identbf_sb", [128, 128], BF16)
        mask = sbt("mask_sb", [128, 128])
        ones_row = sbt("ones_sb", [1, 128])
        scx = sbt("scx", [1, 1]); scd_ = sbt("scd", [1, 1])
        sig_cx = sbt("sigcx", [1, 1]); sig_cd = sbt("sigcd", [1, 1])
        lmax = sbt("lmax", [1, 1])
        gmax = [sbt(f"gmax{i}", [1, 1]) for i in range(2)]
        tmp11 = [sbt(f"t11_{i}", [1, 1]) for i in range(4)]
        sx_bc = sbt("sx_bc", [128, 1]); rsx_bc = sbt("rsx_bc", [128, 1])
        s2_bc = sbt("s2_bc", [128, 1]); rq2_bc = sbt("rq2_bc", [128, 1])
        bfw = max(2 * FC * FBLK + 4 * FC * 128,
                  2 * IC * FSP + 2 * IC * 128, 9216 + 32)
        bfa = sbt("bfa", [128, bfw], BF16)
        astg = [sbt(f"astg{i}", [128, 64]) for i in range(2)]
        i32t = sbt("i32t", [128, 1024], mybir.dt.int32)

        NXQ = 3 if 2 * FC * FBLK + 4 * FC * 128 + FC * FBLK <= bfw else 2
        xqT = [bfa[:, i * FC * FBLK:(i + 1) * FC * FBLK].rearrange(
            "p (a b) -> p a b", a=FC) for i in range(NXQ)]
        wslg = bfa[:, NXQ * FC * FBLK:NXQ * FC * FBLK + 4 * FC * 128].rearrange(
            "p (a b c) -> p a b c", a=2, c=128)
        NHQ = 3 if 3 * IC * FSP + 2 * IC * 128 <= bfw else 2
        hqT = [bfa[:, i * IC * FSP:(i + 1) * IC * FSP].rearrange(
            "p (a b) -> p a b", a=IC) for i in range(NHQ)]
        wsld = bfa[:, NHQ * IC * FSP:NHQ * IC * FSP + 2 * IC * 128].rearrange(
            "p (a b c) -> p a b c", a=2, c=128)
        wqstg = bfa[:, 0:1024]
        qstg = bfa[:, 8192:9216] if bfw >= 9216 else bfa[:, 1024:2048]

        pw = [arena[:, i * DFF:(i + 1) * DFF] for i in range(2)]
        wo_view = xT
        outpre = arena[:, 0:4 * D].rearrange("p (a b) -> p a b", a=4)

        p_big = [ent(nc.psum_tensor(f"pb{i}", [128, 512], F32)) for i in range(2)]
        p_pv = ent(nc.psum_tensor("ppv", [128, KC * 128], F32))
        p_t = ent(nc.psum_tensor("pt", [128, 512], F32))
        p_misc = ent(nc.psum_tensor("pm", [128, 512], F32))

        sems = {nm: ent(nc.semaphore(nm)) for nm in p.sem_names}

        pb_last = {0: None, 1: None}
        pt_last = {k: None for k in range(4)}
        pm_last = [None]

        def _ap(x):
            return x if isinstance(x, bass.AP) else x[:]

        def mm(out, lhsT, rhs, start, stop, waits=(), skip=False):
            out, lhsT, rhs = _ap(out), _ap(lhsT), _ap(rhs)
            return p.op(T, lambda e, o=out, l=lhsT, r=rhs, s1=start, s2=stop,
                        sk=skip: e.matmul(o, l, r, start=s1, stop=s2,
                                          skip_group_check=sk),
                        waits=waits, inc=ts)

        def tr_(out, in_, waits=()):
            out, in_ = _ap(out), _ap(in_)
            idn = ident_bf[:] if in_.dtype == BF16 else ident[:]
            return p.op(T, lambda e, o=out, i=in_, n=idn: e.transpose(o, i, n),
                        waits=waits, inc=ts)

        def vcopy(out, in_, waits=()):
            out, in_ = _ap(out), _ap(in_)
            return p.op(V, lambda e, o=out, i=in_: e.tensor_copy(out=o, in_=i),
                        waits=waits, inc=vs)

        def vts(out, in0, s1, s2=None, op0=OP.mult, op1=None, waits=()):
            out, in0 = _ap(out), _ap(in0)
            s1 = s1 if isinstance(s1, (int, float)) else _ap(s1)
            if op1 is None:
                return p.op(V, lambda e, o=out, i=in0, a=s1, c0=op0:
                            e.tensor_scalar(out=o, in0=i, scalar1=a, scalar2=None,
                                            op0=c0), waits=waits, inc=vs)
            return p.op(V, lambda e, o=out, i=in0, a=s1, b=s2, c0=op0, c1=op1:
                        e.tensor_scalar(out=o, in0=i, scalar1=a, scalar2=b,
                                        op0=c0, op1=c1), waits=waits, inc=vs)

        def vtt(out, in0, in1, op, waits=()):
            out, in0, in1 = _ap(out), _ap(in0), _ap(in1)
            return p.op(V, lambda e, o=out, a=in0, b=in1, c=op:
                        e.tensor_tensor(out=o, in0=a, in1=b, op=c),
                        waits=waits, inc=vs)

        def vred(out, in_, op=OP.max, absv=False, waits=()):
            out, in_ = _ap(out), _ap(in_)
            return p.op(V, lambda e, o=out, i=in_, c=op, av=absv:
                        e.tensor_reduce(out=o, in_=i, axis=AXX, op=c,
                                        apply_absolute_value=av),
                        waits=waits, inc=vs)

        def vrecip(out, in_, waits=()):
            out, in_ = _ap(out), _ap(in_)
            return p.op(V, lambda e, o=out, i=in_: e.reciprocal(out=o, in_=i),
                        waits=waits, inc=vs)

        def act(out, in_, fn, bias=None, scale=1.0, waits=()):
            out, in_ = _ap(out), _ap(in_)
            if bias is None:
                return p.op(A, lambda e, o=out, i=in_, f=fn, sc=scale:
                            e.activation(out=o, in_=i, func=f, scale=sc),
                            waits=waits, inc=as_)
            b_ap = _ap(bias)
            return p.op(A, lambda e, o=out, i=in_, f=fn, b=b_ap, sc=scale:
                        e.activation(out=o, in_=i, func=f, bias=b, scale=sc),
                        waits=waits, inc=as_)

        def vstt(out, in0, scalar, in1, op0, op1, waits=()):
            out, in0, in1 = _ap(out), _ap(in0), _ap(in1)
            scalar = scalar if isinstance(scalar, (int, float)) else _ap(scalar)
            return p.op(V, lambda e, o=out, a=in0, s=scalar, b=in1, c0=op0,
                        c1=op1: e.scalar_tensor_tensor(
                            out=o, in0=a, scalar=s, in1=b, op0=c0, op1=c1),
                        waits=waits, inc=vs)

        def acopy(out, in_, waits=()):
            out, in_ = _ap(out), _ap(in_)
            return p.op(A, lambda e, o=out, i=in_: e.copy(out=o, in_=i),
                        waits=waits, inc=as_)

        def vmemset(ap_, val, waits=()):
            ap_ = _ap(ap_)
            return p.op(V, lambda e, a=ap_, v=val: e.memset(a, v),
                        waits=waits, inc=vs)

        def tslot(slot):
            t_ = p_t if slot % 2 == 0 else p_misc
            off = (slot // 2) * 128
            return t_[:, off:off + 128]

        def transpose_to(slot, dst, src, waits=()):
            sl = tslot(slot)
            other = (slot + 2) % 4
            tk = tr_(sl, src, waits=tuple(waits) +
                     (pt_last[slot], pt_last[other], tk_i, tk_ibf))
            tv = vcopy(dst, sl, waits=(tk,))
            pt_last[slot] = tv
            return tv

        def quant_chain(dst_bf, src, rq_ap, tmp_a, tmp_b, tmp_c, i32_ap,
                        waits=()):
            p.drain(V)
            tu = vts(tmp_a, src, rq_ap, 0.5, OP.mult, OP.add, waits=waits)
            t1 = acopy(i32_ap, tmp_a, waits=(tu,))
            p.drain(A)
            t2 = acopy(tmp_b, i32_ap)
            td = vtt(tmp_c, tmp_a, tmp_b, OP.subtract, waits=(t2,))
            p.drain(V)
            vstt(tmp_b, tmp_c, -1.0, tmp_b, OP.is_ge, OP.add)
            p.drain(V)
            vstt(tmp_b, tmp_c, 0.0, tmp_b, OP.is_ge, OP.add)
            p.drain(V)
            vstt(tmp_b, tmp_c, 1.0, tmp_b, OP.is_ge, OP.add)
            p.drain(V)
            vts(tmp_a, tmp_b, -2.0, 127.0, OP.add, OP.min)
            p.drain(V)
            return vts(dst_bf, tmp_a, -128.0, None, OP.max)

        def ln_chain(src, y_out, tc_b, sq_b, waits=(), prev=None):
            tc_b = _ap(tc_b)[:, 0:D]
            sq_b = _ap(sq_b)[:, 0:D]
            w0 = tuple(waits) + ((prev,) if prev else ())
            p.drain(V)
            t0 = vred(ssm[0], src, OP.add, waits=w0)
            p.drain(V)
            t1 = vts(ssm[1], ssm[0], -1.0 / D, None, OP.mult)
            p.drain(V)
            tA = vts(tc_b, src, ssm[1][:, 0:1], None, OP.add, waits=w0)
            tB = act(sq_b, tc_b, AF.Square, waits=(tA,))
            t2 = vred(ssm[2], sq_b, OP.add, waits=(tB,))
            tC = act(ssm[3], ssm[2], AF.Sqrt, bias=consts[:, CB_EPS:CB_EPS + 1],
                     scale=1.0 / D, waits=(t2,) + ((prev,) if prev else ()))
            t3 = vrecip(ssm[4], ssm[3], waits=(tC,))
            p.drain(V)
            t4 = vts(y_out, tc_b, ssm[4][:, 0:1], None, OP.mult, waits=(tA,))
            return t4, tA

        p.dma(S, ld_c, consts[:], consts_d)
        p.dma(S, ld_c, ident[:], ident_d)
        p.dma(S, ld_c, mask[:], mask_d)
        p.dma(S, ld_c, ones_row[:], ones_d)
        p.dma(S, ld_c, scx[:], clipx_d)
        tk_in = p.dma(S, ld_c, scd_[:], clipd_d)
        tk_i = tk_m = tk_o = tk_in

        for _base in (0, 2 * SL):
            vmemset(qt[64:128, _base:_base + SL], 0.0)
            vmemset(qt[0:64, _base + SL:_base + 2 * SL], 0.0)
            vmemset(kt[64:128, _base:_base + SL], 0.0)
            vmemset(kt[0:64, _base + SL:_base + 2 * SL], 0.0)
        for kc in range(KC):
            vmemset(v65[:, kc, 64:65], 1.0)
            vmemset(v65[:, kc, 129:130], 1.0)
        tk_ibf = vcopy(ident_bf, ident, waits=(tk_i,))
        tk_sig1 = act(sig_cx[:], scx[:], AF.Sigmoid, waits=(tk_in,))
        p.drain(A)
        tk_sig2 = act(sig_cd[:], scd_[:], AF.Sigmoid, waits=(tk_in,))

        stg_tok = {0: None, 1: None}
        stg_i = [0]

        def stage_store(src_slot_tv_fn, dst, bf):
            b = stg_i[0] % 2
            stg_i[0] += 1
            stg = (wqstg[:, b * 512:b * 512 + 128] if bf
                   else s512[4 + b][:, 0:128])
            tv = src_slot_tv_fn(stg, stg_tok[b])
            td = p.dma(G, st_p[b], dst, stg, waits=(tv,))
            stg_tok[b] = td
            return td

        pw_tok = [None, None]
        for wi in range(4):
            for oc in range(FC):
                b = oc % 2
                ld = p.dma(S, ld_p[b], pw[b][:, 0:D],
                           w_attn_d[wi][oc * 128:oc * 128 + 128, :],
                           waits=(pw_tok[b],))
                for fc in range(FC):
                    slot = fc % 4
                    sl = tslot(slot)
                    tk = tr_(sl, pw[b][:, fc * 128:fc * 128 + 128],
                             waits=(ld, pt_last[slot], pt_last[(slot + 2) % 4],
                                    tk_i))

                    def mk(sl_=sl, tk_=tk):
                        def f(stg, st_tok):
                            return vcopy(stg, sl_, waits=(tk_, st_tok))
                        return f
                    tv_holder = {}

                    def fn(stg, st_tok, sl_=sl, tk_=tk, hold=tv_holder):
                        tv = vcopy(stg, sl_, waits=(tk_, st_tok))
                        hold["tv"] = tv
                        return tv
                    stage_store(fn, wt_scr[wi][fc, oc], bf=False)
                    pt_last[fc % 4] = tv_holder["tv"]
                pw_tok[b] = (ts, p.cnt[ts])

        def prep_quant_weight(w_dram, n_oc, in_dim, dst_fn, scale_sb):
            stage_q = [None]
            for oc in range(n_oc):
                b = oc % 2
                ld = p.dma(S, ld_p[b], pw[b][:, 0:in_dim],
                           w_dram[oc * 128:oc * 128 + 128, :],
                           waits=(pw_tok[b],))
                rm = vred(ssm[5], pw[b][:, 0:in_dim], OP.max, absv=True,
                          waits=(ld,))
                p.drain(V)
                vts(scale_sb[:, oc:oc + 1], ssm[5][:, 0:1], 1.0 / 127.0, None,
                    OP.mult)
                vrecip(ssm[6], ssm[5])
                p.drain(V)
                t3 = vts(ssm[7], ssm[6][:, 0:1], 127.0, None, OP.mult)
                for seg in range(0, in_dim, 1024):
                    wdt = min(1024, in_dim - seg)
                    tq = quant_chain(s1k[3][:, 0:wdt], pw[b][:, seg:seg + wdt],
                                     ssm[7][:, 0:1], s1k[4][:, 0:wdt],
                                     s1k[5][:, 0:wdt], s1k[2][:, 0:wdt],
                                     i32t[:, 0:wdt], waits=(t3, stage_q[0]))
                    for k in range(wdt // 128):
                        fc = (seg + k * 128) // 128
                        slot = fc % 4
                        sl = tslot(slot)
                        tk = tr_(sl, s1k[3][:, k * 128:k * 128 + 128],
                                 waits=(tq, pt_last[slot],
                                        pt_last[(slot + 2) % 4], tk_i))
                        hold = {}

                        def fn(stg, st_tok, sl_=sl, tk_=tk, h=hold):
                            tv = vcopy(stg, sl_, waits=(tk_, st_tok))
                            h["tv"] = tv
                            return tv
                        stage_store(fn, dst_fn(fc, oc), bf=True)
                        pt_last[slot] = hold["tv"]
                    stage_q[0] = (ts, p.cnt[ts])
                pw_tok[b] = (vs, p.cnt[vs])

        prep_quant_weight(wg_d, IC, D, lambda fc, oc: wgq_scr[fc, oc], sg_sb)
        prep_quant_weight(wu_d, IC, D, lambda fc, oc: wuq_scr[fc, oc], su_sb)
        prep_quant_weight(wd_d, FC, DFF, lambda fc, oc: wdq_scr[fc, oc], sd_sb)

        prep_done2 = ((st_p[0], p.cnt[st_p[0]]), (st_p[1], p.cnt[st_p[1]]))
        p.barrier(bar)

        for s_i in range(NSEQ):
            xt_bufs = [s1k[0], s1k[1]]
            xt_tok = [None, None]
            for tt in range(TT):
                b = tt % 2
                ld = p.dma(S, ld_x[b], xt_bufs[b][:, 0:D],
                           x_d[s_i, tt * 128:tt * 128 + 128, :],
                           waits=(xt_tok[b],))
                for fc in range(FC):
                    transpose_to(fc % 4, xT[:, fc, tt * 128:tt * 128 + 128],
                                 xt_bufs[b][:, fc * 128:fc * 128 + 128],
                                 waits=(ld,))
                xt_tok[b] = (ts, p.cnt[ts])
            xT_ready = (vs, p.cnt[vs])

            ws_tok = [None, None]
            head_done = [None]
            for hp in range(NHP):
                qk_evacs = []
                for pi, name in ((0, "q"), (1, "k"), (2, "v")):
                    b = (hp * 3 + pi) % 2
                    wld = p.dma(S, ld_w[b], ws[:, b, 0:FC, :],
                                wt_scr[pi][:, hp].rearrange("a p c -> p a c"),
                                waits=(ws_tok[b],) + prep_done2)
                    if name in ("q", "k"):
                        dst = qt if name == "q" else kt
                        for st_ in range(SL // OSW):
                            pb = st_ % 2
                            tk = None
                            for fc in range(FC):
                                tk = mm(p_big[pb][:, 0:OSW], ws[:, b, fc, :],
                                        xT[:, fc, st_ * OSW:st_ * OSW + OSW],
                                        start=(fc == 0), stop=(fc == FC - 1),
                                        waits=(wld, xT_ready, pb_last[pb])
                                        if fc == 0 else ())
                            c0 = st_ * OSW
                            vcopy(dst[0:64, c0:c0 + OSW],
                                  p_big[pb][0:64, 0:OSW], waits=(tk,))
                            vcopy(dst[64:128, SL + c0:SL + c0 + OSW],
                                  p_big[pb][64:128, 0:OSW], waits=())
                            p.drain(V)
                            vtt(dst[0:64, 2 * SL + c0:2 * SL + c0 + OSW],
                                p_big[pb][0:64, 0:OSW],
                                dst[0:64, c0:c0 + OSW], OP.subtract)
                            tv1 = vtt(dst[64:128,
                                          3 * SL + c0:3 * SL + c0 + OSW],
                                      p_big[pb][64:128, 0:OSW],
                                      dst[64:128, SL + c0:SL + c0 + OSW],
                                      OP.subtract)
                            pb_last[pb] = tv1
                            qk_evacs.append(tv1)
                        ws_tok[b] = (ts, p.cnt[ts])
                    else:
                        for gt in range(TT // 4):
                            pb = gt % 2
                            tk = None
                            for t4 in range(4):
                                tt = gt * 4 + t4
                                for fc in range(FC):
                                    tk = mm(p_big[pb][:, t4 * 128:t4 * 128 + 128],
                                            xT[:, fc, tt * 128:tt * 128 + 128],
                                            ws[:, b, fc, :],
                                            start=(t4 == 0 and fc == 0),
                                            stop=(t4 == 3 and fc == FC - 1),
                                            waits=(wld, xT_ready, pb_last[pb])
                                            if (t4 == 0 and fc == 0) else (),
                                            skip=True)
                            tvl = None
                            for t4 in range(4):
                                tt = gt * 4 + t4
                                vcopy(v65[:, tt, 0:64],
                                      p_big[pb][:, t4 * 128:t4 * 128 + 64],
                                      waits=(tk,))
                                tvl = vcopy(v65[:, tt, 65:129],
                                            p_big[pb][:, t4 * 128 + 64:
                                                       t4 * 128 + 128])
                            pb_last[pb] = tvl
                            qk_evacs.append(tvl)
                        ws_tok[b] = (ts, p.cnt[ts])
                qk_ready = (vs, p.cnt[vs])

                for h in range(2):
                    g = 2 * hp + h
                    col0 = h * SL
                    prev_head = head_done[0]
                    pv_tok = {}
                    for kc in range(KC):
                        eb = kc % 2
                        q0 = kc * 128
                        exps = []
                        while q0 < SL:
                            wdt = min(512, SL - q0)
                            pb = len(exps) % 2
                            w0 = (qk_ready, pb_last[pb], pv_tok.get(kc - 2))
                            kcol = col0 + kc * 128
                            qcol = col0 + q0
                            tkm = mm(p_big[pb][:, 0:wdt],
                                     kt[:, kcol:kcol + 128],
                                     qt[:, qcol:qcol + wdt],
                                     start=True, stop=False, waits=w0,
                                     skip=True)
                            tkm = mm(p_big[pb][:, 0:wdt],
                                     kt[:, kcol:kcol + 128],
                                     qt[:, 2 * SL + qcol:2 * SL + qcol + wdt],
                                     start=False, stop=False, skip=True)
                            tkm = mm(p_big[pb][:, 0:wdt],
                                     kt[:, 2 * SL + kcol:2 * SL + kcol + 128],
                                     qt[:, qcol:qcol + wdt],
                                     start=False, stop=True, skip=True)
                            if q0 == kc * 128:
                                tkm = vtt(p_big[pb][:, 0:128], p_big[pb][:, 0:128],
                                          mask[:], OP.add, waits=(tkm, tk_m))
                            tke = act(e_sb[eb][:, q0 - kc * 128:
                                               q0 - kc * 128 + wdt],
                                      p_big[pb][:, 0:wdt], AF.Exp,
                                      bias=consts[:, CB_EXP:CB_EXP + 1],
                                      scale=0.125, waits=(tkm,))
                            pb_last[pb] = tke
                            exps.append(tke)
                            q0 += wdt
                        tkpv = None
                        for qi in range(kc, KC):
                            w0 = tuple(exps) if qi == kc else ()
                            if qi == kc and kc == 0 and prev_head is not None:
                                w0 = w0 + (prev_head,)
                            tkpv = mm(p_pv[:, qi * 128:qi * 128 + 65],
                                      e_sb[eb][:, (qi - kc) * 128:
                                               (qi - kc) * 128 + 128],
                                      v65[:, kc, h * 65:h * 65 + 65],
                                      start=(kc == 0 and qi % 4 == 0),
                                      stop=(qi == kc), waits=w0, skip=True)
                        pv_tok[kc] = tkpv
                    for qi in range(KC):
                        ab = qi % 2
                        p.drain(V)
                        vrecip(ssm[0], p_pv[:, qi * 128 + 64:qi * 128 + 65],
                               waits=(pv_tok[KC - 1], stg_tok[ab]))
                        p.drain(V)
                        tv = vts(astg[ab][:], p_pv[:, qi * 128:qi * 128 + 64],
                                 ssm[0][:, 0:1], None, OP.mult)
                        td = p.dma(G, st_at[ab], attn_scr[g, qi], astg[ab][:],
                                   waits=(tv,))
                        stg_tok[ab] = td
                    head_done[0] = (vs, p.cnt[vs])
            attn_done2 = ((st_at[0], p.cnt[st_at[0]]),
                          (st_at[1], p.cnt[st_at[1]]))
            p.barrier(bar)

            wo_ld = None
            for fc in range(FC):
                wo_ld = p.dma(S, ld_w[0], wo_view[:, fc, 0:D],
                              wt_scr[3][fc].rearrange("c p d -> p c d"),
                              waits=prep_done2 + (ws_tok[0], ws_tok[1])
                              if fc == 0 else ())
            asm_bufs = [a2gap[0], a2gap[1]]
            att_bufs = [a2gap[2], a2gap[3]]
            xt2_bufs = [a2gap[4], a2gap[5]]
            r1pre, tcb, sqb, r1out = s1k[2], s1k[3], s1k[4], s1k[5]
            btok = {}
            prev_ln = None
            for tt in range(TT):
                b = tt % 2
                lda = p.dma(S, ld_at[b], asm_bufs[b][:, 0:NH * 64],
                            attn_scr[:, tt].rearrange("a p c -> p a c"),
                            waits=attn_done2 + (btok.get(("asm", b)),))
                ldx = p.dma(S, ld_x[b], xt2_bufs[b][:, 0:D],
                            x_d[s_i, tt * 128:tt * 128 + 128, :],
                            waits=(btok.get(("xt2", b)),))
                trv = None
                for fc in range(FC):
                    trv = transpose_to(fc % 4,
                                       att_bufs[b][:, fc * 128:fc * 128 + 128],
                                       asm_bufs[b][:, fc * 128:fc * 128 + 128],
                                       waits=(lda, btok.get(("att", b))))
                ot = []
                for os_ in range(D // OSW):
                    pb = os_ % 2
                    tk = None
                    for fc in range(FC):
                        tk = mm(p_big[pb][:, 0:OSW],
                                att_bufs[b][:, fc * 128:fc * 128 + 128],
                                wo_view[:, fc, os_ * OSW:os_ * OSW + OSW],
                                start=(fc == 0), stop=(fc == FC - 1),
                                waits=(trv, wo_ld, pb_last[pb])
                                if fc == 0 else ())
                    tv = vtt(r1pre[:, os_ * OSW:os_ * OSW + OSW],
                             p_big[pb][:, 0:OSW],
                             xt2_bufs[b][:, os_ * OSW:os_ * OSW + OSW],
                             OP.add, waits=(tk, ldx, btok.get(("r1p",))))
                    pb_last[pb] = tv
                    ot.append(tv)
                btok[("att", b)] = (ts, p.cnt[ts])
                tln, tA = ln_chain(r1pre[:, 0:D], r1out[:, 0:D], tcb, sqb,
                                   waits=tuple(ot) + (btok.get(("r1o",)),),
                                   prev=prev_ln)
                prev_ln = tln
                btok[("r1p",)] = tA
                tvm = vred(rmax_all[:, s_i * TT + tt:s_i * TT + tt + 1],
                           r1out[:, 0:D], OP.max, absv=True, waits=(tln,))
                td = p.dma(G, st_r1, r1_scr[s_i * TT + tt], r1out[:, 0:D],
                           waits=(tvm,))
                btok[("r1o",)] = td
                btok[("asm", b)] = (ts, p.cnt[ts])
                btok[("xt2", b)] = (vs, p.cnt[vs])
            p.barrier(bar)

        r1_done = (st_r1, p.cnt[st_r1])

        def global_max_and_scales(src_cols, ncols, sig_tok, sig_sb, cc_i, cc_o,
                                  g_sb, bc_a, bc_b, scls):
            p.drain(V)
            t0 = vred(ssm[0], src_cols[:, 0:ncols], OP.max)
            tt0 = tr_(p_misc[0:1, 0:128], ssm[0][:, 0:1],
                      waits=(t0, tk_i, pm_last[0]))
            t1 = vred(lmax[:], p_misc[0:1, 0:128], OP.max, waits=(tt0,))
            td1 = p.dma(G, ccd, cc_i, lmax[:], waits=(t1,))
            tcc = p.op(G, lambda e: e.collective_compute(
                "AllReduce", OP.max, replica_groups=[list(range(cfg.NC))],
                ins=[cc_i.opt()], outs=[cc_o.opt()]), waits=(td1,), inc=ccch)
            td2 = p.dma(G, ccd, g_sb[:], cc_o, waits=(tcc,))
            vtt(tmp11[0][:], sig_sb[:], g_sb[:], OP.mult, waits=(td2, sig_tok))
            p.drain(V)
            ts1 = vts(tmp11[1][:], tmp11[0][:], 1.0 / 127.0, None, OP.mult)
            vrecip(tmp11[2][:], tmp11[0][:])
            p.drain(V)
            ts2 = vts(tmp11[3][:], tmp11[2][:], 127.0, None, OP.mult)
            tb1 = mm(p_misc[:, 128:129], ones_row[:], tmp11[1][:], True, True,
                     waits=(ts1, tk_o, t1))
            tv1 = vcopy(bc_a[:], p_misc[:, 128:129], waits=(tb1,))
            tb2 = mm(p_misc[:, 256:257], ones_row[:], tmp11[3][:], True, True,
                     waits=(ts2, tv1))
            tv2 = vcopy(bc_b[:], p_misc[:, 256:257], waits=(tb2,))
            pm_last[0] = tv2
            p.drain(V)
            for dst, src in scls:
                vts(dst, src, bc_a[:, 0:1], None, OP.mult, waits=(tv1, tv2))
            p.drain(V)

        global_max_and_scales(rmax_all, NT16, tk_sig1, sig_cx, cc_io[0],
                              cc_io[1], gmax[0], sx_bc, rsx_bc,
                              [(gsc_sb[:, 0:IC], sg_sb[:, 0:IC]),
                               (usc_sb[:, 0:IC], su_sb[:, 0:IC])])
        p.barrier(bar)

        hcol = [0]
        r1b = [s1k[0], s1k[1]]
        rb_tok = [None, None]
        wsl_tok = [None, None]
        st_tok = {}
        xq_last_mm = [None] * NXQ
        xq_done = [None, None]
        xq_ready_buf = [None] * NXQ

        def f1_stage(blk):
            b2 = blk % 2
            bq = blk % NXQ
            xcol = b2 * FBLK
            ntt = FBLK // 128
            for k in range(ntt):
                b = (blk * ntt + k) % 2
                ld = p.dma(S, ld_r1[b], r1b[b][:, 0:D], r1_scr[blk * ntt + k],
                           waits=(r1_done, rb_tok[b]))
                for fc in range(FC):
                    transpose_to(fc % 4,
                                 xT[:, fc, xcol + k * 128:xcol + k * 128 + 128],
                                 r1b[b][:, fc * 128:fc * 128 + 128],
                                 waits=(ld, xq_done[b2]))
                rb_tok[b] = (ts, p.cnt[ts])
            xTf_ready = (vs, p.cnt[vs])
            tq = None
            for fc in range(FC):
                tq = quant_chain(xqT[bq][:, fc, 0:FBLK],
                                 xT[:, fc, xcol:xcol + FBLK],
                                 rsx_bc[:, 0:1], s1k[2][:, 0:FBLK],
                                 s1k[3][:, 0:FBLK], s1k[4][:, 0:FBLK],
                                 i32t[:, 0:FBLK],
                                 waits=(xTf_ready, xq_last_mm[bq]))
            xq_done[b2] = tq
            xq_ready_buf[bq] = tq

        ppv_last = {0: None, 1: None}
        for blk in range(min(2, NBLK)):
            f1_stage(blk)
        bp = 0
        while 2 * bp < NBLK:
            blocks = [2 * bp] + ([2 * bp + 1] if 2 * bp + 1 < NBLK else [])
            if 2 * bp + 2 < NBLK and NXQ > 2:
                f1_stage(2 * bp + 2)
            for ic in range(IC):
                b = ic % 2
                wld = p.dma(S, ld_w[b], wslg[:, b, 0:FC, :],
                            wgq_scr[:, ic].rearrange("a p c -> p a c"),
                            waits=(wsl_tok[b],) + prep_done2)
                wld = p.dma(S, ld_w[b], wslg[:, b, FC:2 * FC, :],
                            wuq_scr[:, ic].rearrange("a p c -> p a c"))
                for j, blk in enumerate(blocks):
                    bq = blk % NXQ
                    pg = p_big[0][:, 0:FBLK] if j == 0 else p_pv[:, 0:FBLK]
                    pu = (p_big[1][:, 0:FBLK] if j == 0
                          else p_pv[:, FBLK:2 * FBLK])
                    gl = pb_last[0] if j == 0 else ppv_last[0]
                    ul = pb_last[1] if j == 0 else ppv_last[1]
                    tkg = tku = None
                    for fc in range(FC):
                        tkg = mm(pg, wslg[:, b, fc, :],
                                 xqT[bq][:, fc, 0:FBLK],
                                 start=(fc == 0), stop=(fc == FC - 1),
                                 waits=(wld, xq_ready_buf[bq], gl)
                                 if fc == 0 else ())
                    for fc in range(FC):
                        tku = mm(pu, wslg[:, b, FC + fc, :],
                                 xqT[bq][:, fc, 0:FBLK],
                                 start=(fc == 0), stop=(fc == FC - 1),
                                 waits=(ul,) if fc == 0 else ())
                    tas = act(s512[j][:, 0:FBLK], pg, AF.Silu,
                              scale=gsc_sb[:, ic:ic + 1],
                              waits=(tkg, st_tok.get(("h", j))))
                    tvu = vts(s512[2 + j][:, 0:FBLK], pu,
                              usc_sb[:, ic:ic + 1], None, OP.mult,
                              waits=(tku, st_tok.get(("h", j))))
                    if j == 0:
                        pb_last[0] = tas
                        pb_last[1] = tvu
                    else:
                        ppv_last[0] = tas
                        ppv_last[1] = tvu
                    hst = s512[4 + j]
                    tvh = vtt(hst[:, 0:FBLK], s512[j][:, 0:FBLK],
                              s512[2 + j][:, 0:FBLK],
                              OP.mult, waits=(tas, tvu,
                                              st_tok.get(("hst", j))))
                    st_tok[("h", j)] = tvh
                    tvm = vred(hmax_all[:, hcol[0]:hcol[0] + 1],
                               hst[:, 0:FBLK],
                               OP.max, absv=True, waits=(tvh,))
                    hcol[0] += 1
                    tdh = p.dma(G, st_h[j],
                                h_scr[ic][:, blk * FBLK:blk * FBLK + FBLK],
                                hst[:, 0:FBLK], waits=(tvm,))
                    st_tok[("hst", j)] = tdh
                wsl_tok[b] = (ts, p.cnt[ts])
            for blk in blocks:
                xq_last_mm[blk % NXQ] = (ts, p.cnt[ts])
            if 2 * bp + 3 < NBLK and NXQ > 2:
                f1_stage(2 * bp + 3)
            elif 2 * bp + 2 < NBLK and NXQ == 2:
                f1_stage(2 * bp + 2)
                if 2 * bp + 3 < NBLK:
                    f1_stage(2 * bp + 3)
            bp += 1
        p.barrier(bar)
        h_done2 = ((st_h[0], p.cnt[st_h[0]]), (st_h[1], p.cnt[st_h[1]]))

        global_max_and_scales(hmax_all, hcol[0], tk_sig2, sig_cd, cc_io[2],
                              cc_io[3], gmax[1], s2_bc, rq2_bc,
                              [(dsc_sb[:, 0:FC], sd_sb[:, 0:FC])])
        p.barrier(bar)

        NTK = FSP // 128
        hlb = [s1k[0], s1k[1]]
        hb_tok = [None, None]
        wdl_tok = [None, None]
        dstg = [s512[4], s512[5]]
        dstg_tok = [None, None]
        hq_last_mm = [None, None]
        rb_tok = {}
        prev_ln2 = None
        r1b = [f2v[5], f2v[6]]
        yb = [f2v[0], f2v[1]]
        r2pre, tcb, sqb = f2v[2], f2v[3], f2v[4]
        hq_ready_buf = [None] * NHQ
        hq_last_mm = [None] * NHQ
        f2_pair = [0]

        def f2_stage(sp):
            bq = sp % NHQ
            tq = None
            ic = 0
            while ic < IC:
                npair = min(1024 // FSP, IC - ic)
                wdt = FSP * npair
                b = f2_pair[0] % 2
                f2_pair[0] += 1
                ld = None
                for j in range(npair):
                    ld = p.dma(S, ld_h[b], hlb[b][:, j * FSP:j * FSP + FSP],
                               h_scr[ic + j][:, sp * FSP:sp * FSP + FSP],
                               waits=h_done2 + (hb_tok[b],) if j == 0 else ())
                tq = quant_chain(
                    hqT[bq][:].rearrange("p a b -> p (a b)")[:, ic * FSP:
                                                             ic * FSP + wdt],
                    hlb[b][:, 0:wdt], rq2_bc[:, 0:1],
                    s1k[2][:, 0:wdt], s1k[3][:, 0:wdt], s1k[4][:, 0:wdt],
                    i32t[:, 0:wdt], waits=(ld, hq_last_mm[bq]))
                hb_tok[b] = tq
                ic += npair
            hq_ready_buf[bq] = tq

        for sp in range(min(2, NSP)):
            f2_stage(sp)
        bp = 0
        while 2 * bp < NSP:
            strips = [2 * bp] + ([2 * bp + 1] if 2 * bp + 1 < NSP else [])
            if 2 * bp + 2 < NSP and NHQ > 2:
                f2_stage(2 * bp + 2)
            for oc in range(FC):
                b = oc % 2
                wl = p.dma(S, ld_w[b], wsld[:, b, 0:IC, :],
                           wdq_scr[:, oc].rearrange("a p c -> p a c"),
                           waits=(wdl_tok[b],) + prep_done2)
                for j, sp in enumerate(strips):
                    bq = sp % NHQ
                    tk = None
                    for ic in range(IC):
                        tk = mm(p_big[j][:, 0:FSP], wsld[:, b, ic, :],
                                hqT[bq][:, ic, 0:FSP], start=(ic == 0),
                                stop=(ic == IC - 1),
                                waits=(wl, hq_ready_buf[bq], pb_last[j])
                                if ic == 0 else ())
                    tvd = vts(dstg[j][:, 0:FSP], p_big[j][:, 0:FSP],
                              dsc_sb[:, oc:oc + 1], None, OP.mult,
                              waits=(tk, dstg_tok[j]))
                    pb_last[j] = tvd
                    for k in range(NTK):
                        kk = j * NTK + k
                        transpose_to((oc * 2 * NTK + kk) % 4,
                                     outpre[:, kk, oc * 128:oc * 128 + 128],
                                     dstg[j][:, k * 128:k * 128 + 128],
                                     waits=(tvd, rb_tok.get(("opre", kk))))
                    dstg_tok[j] = (ts, p.cnt[ts])
                wdl_tok[b] = (ts, p.cnt[ts])
            for sp in strips:
                hq_last_mm[sp % NHQ] = (ts, p.cnt[ts])
            if 2 * bp + 3 < NSP and NHQ > 2:
                f2_stage(2 * bp + 3)
            elif 2 * bp + 2 < NSP and NHQ == 2:
                f2_stage(2 * bp + 2)
                if 2 * bp + 3 < NSP:
                    f2_stage(2 * bp + 3)
            outpre_ready = (vs, p.cnt[vs])
            for k in range(len(strips) * NTK):
                b = k % 2
                gi = 2 * bp * NTK + k
                ld = p.dma(S, ld_r1[b], r1b[b][:, 0:D], r1_scr[gi],
                           waits=(r1_done, rb_tok.get(("rb", b))))
                tv = vtt(r2pre[:, 0:D], outpre[:, k, 0:D], r1b[b][:, 0:D],
                         OP.add, waits=(ld, outpre_ready, rb_tok.get(("r2",))))
                rb_tok[("opre", k)] = tv
                tln, tA = ln_chain(r2pre[:, 0:D], yb[b][:, 0:D], tcb, sqb,
                                   waits=(tv, rb_tok.get(("yb", b))),
                                   prev=prev_ln2)
                prev_ln2 = tln
                rb_tok[("r2",)] = tA
                s_idx = gi // TT
                t_idx = gi % TT
                td = p.dma(G, st_y[b], y_d[s_idx, t_idx * 128:t_idx * 128 + 128, :],
                           yb[b][:, 0:D], waits=(tln,))
                rb_tok[("yb", b)] = td
                rb_tok[("rb", b)] = tln
            bp += 1
        p.barrier(bar)

        p.op(G, lambda e: e.nop(),
             waits=((st_y[0], p.cnt[st_y[0]]), (st_y[1], p.cnt[st_y[1]])))

        _safe = SAFE_EVERY
        if _safe > 0:
            newops = []
            cnt2 = 0
            bcnt = {e: 0 for e in ENGINES}
            for rec in p.ops:
                newops.append(rec)
                if rec[3] is not None and rec[3][0].startswith("bar_"):
                    bcnt[rec[3][0][4:]] += 1
                cnt2 += 1
                if cnt2 % _safe == 0:
                    vals = {}
                    for e in ENGINES:
                        bcnt[e] += 1
                        vals[e] = bcnt[e]
                        newops.append((e, lambda eng: eng.nop(), (), (bar[e], 1)))
                    for e in ENGINES:
                        w = tuple((bar[o], vals[o]) for o in ENGINES if o != e)
                        newops.append((e, lambda eng: eng.nop(), w, None))
            p.ops = newops
            for e in ENGINES:
                p.cnt[bar[e]] = bcnt[e]
        p.compress((ts, vs, as_))
        p.check()
        _LAST_PLAN[:] = [p]

        with nc.Block() as block:
            def make(eng_name):
                def body(eng):
                    waited = {}
                    for rec in p.ops:
                        if rec[0] != eng_name:
                            continue
                        for s_, v_ in rec[2]:
                            if waited.get(s_, 0) >= v_:
                                continue
                            eng.wait_ge(sems[s_], v_)
                            waited[s_] = v_
                        inst = rec[1](eng)
                        if rec[3] is not None:
                            inst.then_inc(sems[rec[3][0]], rec[3][1])
                return body
            for e_name in ENGINES:
                getattr(block, e_name)(make(e_name))

    return nc


def host_inputs(cfg, x_shard, Wq, Wk, Wv, Wo, clip_x, clip_down,
                w_gate, w_up, w_down):
    consts = np.zeros((128, 8), np.float32)
    consts[:, 0] = -4.0
    consts[:, 1] = 1e-5
    consts[:, 2] = 0.5
    consts[:, 3] = 1.0
    mask = np.where(np.arange(128)[:, None] <= np.arange(128)[None, :],
                    np.float32(0.0), np.float32(-1e9))
    return {
        "x": np.ascontiguousarray(x_shard, dtype=np.float32),
        "wq": np.ascontiguousarray(Wq, dtype=np.float32),
        "wk": np.ascontiguousarray(Wk, dtype=np.float32),
        "wv": np.ascontiguousarray(Wv, dtype=np.float32),
        "wo": np.ascontiguousarray(Wo, dtype=np.float32),
        "wg": np.ascontiguousarray(w_gate, dtype=np.float32),
        "wu": np.ascontiguousarray(w_up, dtype=np.float32),
        "wd": np.ascontiguousarray(w_down, dtype=np.float32),
        "clipx": np.asarray(clip_x, np.float32).reshape(1, 1),
        "clipd": np.asarray(clip_down, np.float32).reshape(1, 1),
        "ident": np.eye(128, dtype=np.float32),
        "mask": np.ascontiguousarray(mask),
        "consts": consts,
        "ones_row": np.ones((1, 128), np.float32),
    }


def kernel(L0_attn_x, L0_attn_Wq, L0_attn_Wk, L0_attn_Wv, L0_attn_Wo,
           L0_ffn_clip_x, L0_ffn_clip_down, L0_ffn_w_gate, L0_ffn_w_up,
           L0_ffn_w_down):
    cfg = Cfg()
    x = np.asarray(L0_attn_x, np.float32)
    per = x.shape[0] // cfg.NC
    nc = build_nc(cfg)
    in_maps = [host_inputs(cfg, x[c * per:(c + 1) * per], L0_attn_Wq,
                           L0_attn_Wk, L0_attn_Wv, L0_attn_Wo, L0_ffn_clip_x,
                           L0_ffn_clip_down, L0_ffn_w_gate, L0_ffn_w_up,
                           L0_ffn_w_down)
               for c in range(cfg.NC)]
    res = run_bass_kernel_spmd(nc, in_maps, core_ids=list(range(cfg.NC)))
    out = np.concatenate(
        [np.asarray(r["y"], np.float32).reshape(per, cfg.S, cfg.D)
         for r in res.results], axis=0)
    return out
